# version 14
# speedup vs baseline: 1.7217x; 1.0064x over previous
import numpy as np
import ml_dtypes
from contextlib import ExitStack
import concourse.bass as bass
import concourse.mybir as mybir
from concourse.bass_utils import run_bass_kernel_spmd

F32 = mybir.dt.float32
BF16 = mybir.dt.bfloat16
U32 = mybir.dt.uint32
AF = mybir.ActivationFunctionType
ALU = mybir.AluOpType
AX = mybir.AxisListType

MAGIC = 3 * 2.0**22
D = 2048
NT = 16
NEG = -1e30


class SemC:
    def __init__(self, h):
        self.h = h
        self.n = 0

    def inc(self, ins, k=1):
        ins.then_inc(self.h, k)
        self.n += k
        return self.n


class Prog:

    def __init__(self):
        self.nc = bass.Bass("TRN2", target_bir_lowering=False)
        self.es = ExitStack()
        self.scope = self.es
        self.pfx = ""

    def push_scope(self):
        self.scope = ExitStack()
        return self.scope

    def pop_scope(self):
        self.scope.close()
        self.scope = self.es

    def sbuf(self, name, shape, dt):
        return self.scope.enter_context(
            self.nc.sbuf_tensor(self.pfx + name, shape, dt))

    def psum(self, name, shape, dt):
        return self.scope.enter_context(
            self.nc.psum_tensor(self.pfx + name, shape, dt))

    def sem(self, name):
        return SemC(self.es.enter_context(self.nc.semaphore(self.pfx + name)))

    def din(self, name, shape, dt=F32):
        return self.nc.dram_tensor(self.pfx + name, shape, dt,
                                   kind="ExternalInput").ap()

    def dint(self, name, shape, dt=BF16):
        return self.nc.dram_tensor(self.pfx + name, shape, dt,
                                   kind="Internal").ap()

    def dout(self, name, shape, dt=F32):
        return self.nc.dram_tensor(self.pfx + name, shape, dt,
                                   kind="ExternalOutput").ap()

    def finish(self):
        if self.scope is not self.es:
            self.pop_scope()
        self.es.close()
        return self.nc




def _bsetup(p, pfx):
    p.pfx = pfx
    p.push_scope()


def _bfin(p, standalone):
    if standalone:
        return p.finish()
    p.pop_scope()
    p.pfx = ""
    return None


class Stream2:

    def __init__(self, p, name, n, n_consumers=1, nb=2):
        self.nb = nb
        self.ld = [p.sem(f"{name}_l{j}") for j in range(nb)]
        self.use = [p.sem(f"{name}_u{j}") for j in range(nb)]
        self.n = n
        self.ncons = n_consumers

    def emit_loads(self, dma, bufs, srcs, pre_waits=()):
        nb = self.nb
        for i in range(self.n):
            if i < nb:
                for (h, v) in pre_waits:
                    dma.wait_ge(h, v)
            else:
                dma.wait_ge(self.use[i % nb].h, self.ncons * (i // nb))
            dma.dma_start(out=bufs[i % nb][:], in_=srcs[i]).then_inc(self.ld[i % nb].h, 16)
            self.ld[i % nb].n += 16

    def wait_tile(self, eng, i):
        eng.wait_ge(self.ld[i % self.nb].h, 16 * (i // self.nb + 1))

    def done_tile(self, ins, i):
        self.use[i % self.nb].inc(ins)


def emit_ln_stats(p, block, xin_ap, xt, st, sqs, ssq, ssum, mcol, rstd,
                  vtmp, vtmp2, s_sc, s_vc, epscol, engine_dma="sync"):
    sec = getattr(block, engine_dma)

    @sec
    def _(dma):
        st.emit_loads(dma, xt, [xin_ap[t * 128:(t + 1) * 128, :] for t in range(16)])

    @block.scalar
    def _(scalar):
        for t in range(16):
            st.wait_tile(scalar, t)
            if t >= 1:
                scalar.wait_ge(st.use[(t - 1) % 2].h,
                               st.ncons * ((t - 1) // 2 + 1))
            ins = scalar.activation(out=sqs[:], in_=xt[t % 2][:], func=AF.Square,
                                    accum_out=ssq[:, t:t + 1])
            st.done_tile(ins, t)

    @block.vector
    def _(vector):
        for t in range(16):
            st.wait_tile(vector, t)
            ins = vector.reduce_sum(ssum[:, t:t + 1], xt[t % 2][:], axis=AX.X)
            st.done_tile(ins, t)
        vector.wait_ge(st.use[0].h, st.use[0].n)
        vector.wait_ge(st.use[1].h, st.use[1].n)
        mc = s_vc.inc(vector.tensor_scalar(mcol[:], ssum[:], 1.0 / D, None, ALU.mult))
        m1 = s_vc.inc(vector.tensor_scalar(vtmp[:], ssq[:], 1.0 / D, None, ALU.mult))
        vector.wait_ge(s_vc.h, m1)
        m2 = s_vc.inc(vector.tensor_mul(vtmp2[:], mcol[:], mcol[:]))
        vector.wait_ge(s_vc.h, m2)
        s_vc.inc(vector.tensor_sub(vtmp[:], vtmp[:], vtmp2[:]))

    var_mark = s_vc.n

    @block.scalar
    def _(scalar):
        scalar.wait_ge(s_vc.h, var_mark)
        s_sc.inc(scalar.activation(out=vtmp[:], in_=vtmp[:], func=AF.Sqrt, bias=epscol))

    sqrt_mark = s_sc.n

    @block.vector
    def _(vector):
        vector.wait_ge(s_sc.h, sqrt_mark)
        s_vc.inc(vector.reciprocal(rstd[:], vtmp[:]))

    return s_vc.n


def build_l1_ffn(p=None, block=None, xin=None, xout=None, pfx=""):
    standalone = p is None
    if standalone:
        p = Prog()
    nc = p.nc
    _bsetup(p, pfx)
    if xin is None:
        xin = p.din("xin", [2048, D])
    wT = p.din("wT", [D, D])
    identb = p.din("identb", [128, 128], BF16)
    if xout is None:
        xout = p.dout("xout", [2048, D])

    xqT = p.sbuf("xqT", [128, NT * 2048], BF16)
    wqT = p.sbuf("wqT", [128, NT * 2048], BF16)
    wt = [p.sbuf(f"wt{i}", [128, 2048], F32) for i in range(2)]
    xt = [p.sbuf(f"xt{i}", [128, 2048], F32) for i in range(2)]
    yt = [p.sbuf(f"yt{i}", [128, 2048], F32) for i in range(2)]
    xn = p.sbuf("xn", [128, 2048], F32)
    vq = p.sbuf("vq", [128, 2048], F32)
    xq = p.sbuf("xq", [128, 2048], BF16)
    sqs = p.sbuf("sqs", [128, 2048], F32)
    ident = p.sbuf("ident", [128, 128], BF16)
    wabs = p.sbuf("wabs", [128, 16], F32)
    wsum = p.sbuf("wsum", [128, 1], F32)
    onecol = p.sbuf("onecol", [128, 1], F32)
    alpha_s = p.sbuf("alpha_s", [1, 1], F32)
    racol = p.sbuf("racol", [128, 1], F32)
    acol = p.sbuf("acol", [128, 1], F32)
    ssq = p.sbuf("ssq", [128, 16], F32)
    ssum = p.sbuf("ssum", [128, 16], F32)
    mcol = p.sbuf("mcol", [128, 16], F32)
    rstd = p.sbuf("rstd", [128, 16], F32)
    vtmp = p.sbuf("vtmp", [128, 16], F32)
    betac = p.sbuf("betac", [128, 16], F32)
    rbetac = p.sbuf("rbetac", [128, 16], F32)
    rmc = p.sbuf("rmc", [128, 16], F32)
    ysum = p.sbuf("ysum", [128, 16], F32)
    ysq = p.sbuf("ysq", [128, 16], F32)
    ym = p.sbuf("ym", [128, 16], F32)
    ysc = p.sbuf("ysc", [128, 16], F32)
    yv1 = p.sbuf("yv1", [128, 16], F32)
    yv2 = p.sbuf("yv2", [128, 16], F32)
    yv2x = p.sbuf("yv2x", [128, 16], F32)
    abcol = p.sbuf("abcol", [128, 16], F32)
    epsc = p.sbuf("epsc", [128, 1], F32)

    ptr = [p.psum("ptr0", [128, 1024], BF16), p.psum("ptr1", [128, 1024], BF16)]
    palpha = p.psum("palpha", [1, 1], F32)
    pbc = p.psum("pbc", [128, 1], F32)
    py = p.psum("py0", [128, 2048], F32)

    stW = Stream2(p, "stW", 16, n_consumers=1)
    stW2 = Stream2(p, "stW2", 16, n_consumers=1)
    stX = Stream2(p, "stX", 16, n_consumers=2)
    stX2 = Stream2(p, "stX2", 16, n_consumers=1)
    stX3 = Stream2(p, "stX3", 16, n_consumers=1)
    s_vc = p.sem("s_vc")
    s_sc = p.sem("s_sc")
    s_tc = p.sem("s_tc")
    s_ev = p.sem("s_ev")
    s_st = p.sem("s_st")
    s_pyf = p.sem("s_pyf")
    s_cst = p.sem("s_cst")

    if block is None:
        block = p.es.enter_context(nc.Block())

    @block.sync
    def _(sync):
        sync.dma_start(out=ident[:], in_=identb[:, :]).then_inc(s_cst.h, 16)
    s_cst.n += 16

    @block.vector
    def _(vector):
        s_vc.inc(vector.memset(epsc[:], 1e-5))
        s_vc.inc(vector.memset(onecol[:], 1.0))

    @block.sync
    def _(sync):
        stW.emit_loads(sync, wt, [wT[i * 128:(i + 1) * 128, :] for i in range(16)])

    @block.vector
    def _(vector):
        for i in range(16):
            stW.wait_tile(vector, i)
            ins = vector.reduce_sum(wabs[:, i:i + 1], wt[i % 2][:],
                                    axis=AX.X, apply_absolute_value=True)
            stW.done_tile(ins, i)
        vector.wait_ge(stW.use[0].h, stW.use[0].n)
        vector.wait_ge(stW.use[1].h, stW.use[1].n)
        s_vc.inc(vector.reduce_sum(wsum[:], wabs[:], axis=AX.X))
    wsum_mark = s_vc.n

    @block.tensor
    def _(tensor):
        tensor.wait_ge(s_vc.h, wsum_mark)
        s_tc.inc(tensor.matmul(palpha[:, :], onecol[:], wsum[:], start=True, stop=True))
    t_mark = s_tc.n

    @block.scalar
    def _(scalar):
        scalar.wait_ge(s_tc.h, t_mark)
        s_sc.inc(scalar.mul(alpha_s[:], palpha[:, :], 1.0 / (2048.0 * 2048.0)))
    a_mark = s_sc.n

    @block.tensor
    def _(tensor):
        tensor.wait_ge(s_sc.h, a_mark)
        s_tc.inc(tensor.matmul(pbc[:, :], onecol[0:1, :].to_broadcast([1, 128]),
                               alpha_s[:, :], start=True, stop=True))
    t2_mark = s_tc.n

    @block.vector
    def _(vector):
        vector.wait_ge(s_tc.h, t2_mark)
        m = s_vc.inc(vector.reciprocal(racol[:], pbc[:, :]))
        vector.wait_ge(s_vc.h, m)
        s_vc.inc(vector.tensor_copy(acol[:], pbc[:, :]))
    alpha_mark = s_vc.n

    @block.gpsimd
    def _(gpsimd):
        gpsimd.wait_ge(s_vc.h, alpha_mark)
        stW2.emit_loads(gpsimd, wt, [wT[i * 128:(i + 1) * 128, :] for i in range(16)])

    @block.vector
    def _(vector):
        for i in range(16):
            stW2.wait_tile(vector, i)
            if i >= 1:
                vector.wait_ge(stW2.use[(i - 1) % 2].h, (i - 1) // 2 + 1)
            m1 = s_vc.inc(vector.tensor_scalar(vq[:], wt[i % 2][:], racol[:, 0:1],
                                               MAGIC, ALU.mult, ALU.add))
            vector.wait_ge(s_vc.h, m1)
            m2 = s_vc.inc(vector.tensor_scalar(xn[:], vq[:], MAGIC, None, ALU.subtract))
            vector.wait_ge(s_vc.h, m2)
            ins = vector.tensor_scalar(wqT[:, i * 2048:(i + 1) * 2048], xn[:],
                                       -1.0, 1.0, ALU.max, ALU.min)
            stW2.done_tile(ins, i)
    wq_mark0, wq_mark1 = stW2.use[0].n, stW2.use[1].n

    ln_mark = emit_ln_stats(p, block, xin, xt, stX, sqs, ssq, ssum,
                            mcol, rstd, vtmp, yv1, s_sc, s_vc, epsc[:, 0:1])

    @block.sync
    def _(sync):
        stX2.emit_loads(sync, xt, [xin[t * 128:(t + 1) * 128, :] for t in range(16)],
                        pre_waits=[(stX.use[0].h, 16), (stX.use[1].h, 16)])

    @block.vector
    def _(vector):
        for t in range(16):
            stX2.wait_tile(vector, t)
            vector.wait_ge(s_vc.h, ln_mark)
            if t >= 1:
                vector.wait_ge(stX2.use[(t - 1) % 2].h, (t - 1) // 2 + 1)
            if t >= 1:
                vector.wait_ge(s_tc.h, 2 + 2 * t)
            m1 = s_vc.inc(vector.tensor_scalar(xn[:], xt[t % 2][:], mcol[:, t:t + 1],
                                               rstd[:, t:t + 1], ALU.subtract, ALU.mult))
            vector.wait_ge(s_vc.h, m1)
            m2 = s_vc.inc(vector.reduce_max(rmc[:, t:t + 1], xn[:], axis=AX.X,
                                            apply_absolute_value=True))
            vector.wait_ge(s_vc.h, m2)
            m3 = s_vc.inc(vector.tensor_scalar(betac[:, t:t + 1], rmc[:, t:t + 1],
                                               1.0 / 127.0, 1e-8, ALU.mult, ALU.max))
            vector.wait_ge(s_vc.h, m3)
            m4 = s_vc.inc(vector.reciprocal(rbetac[:, t:t + 1], betac[:, t:t + 1]))
            vector.wait_ge(s_vc.h, m4)
            m5 = s_vc.inc(vector.tensor_scalar(vq[:], xn[:], rbetac[:, t:t + 1],
                                               MAGIC, ALU.mult, ALU.add))
            vector.wait_ge(s_vc.h, m5)
            ins = vector.tensor_scalar(xq[:], vq[:], MAGIC, None, ALU.subtract)
            stX2.done_tile(ins, t)

    @block.tensor
    def _(tensor):
        tensor.wait_ge(s_cst.h, 16)
        for t in range(16):
            tensor.wait_ge(stX2.use[t % 2].h, t // 2 + 1)
            for g in range(2):
                if t >= 1:
                    tensor.wait_ge(s_ev.h, 2 * (t - 1) + g + 1)
                for j in range(8):
                    ch = g * 8 + j
                    ins = tensor.transpose(ptr[g][:, j * 128:(j + 1) * 128],
                                           xq[:, ch * 128:(ch + 1) * 128], ident[:])
                s_tc.inc(ins)

    @block.scalar
    def _(scalar):
        for t in range(16):
            for g in range(2):
                scalar.wait_ge(s_tc.h, 2 + 2 * t + g + 1)
                dst = xqT[:, :].rearrange("p (c n) -> p c n", c=16)
                ins = scalar.copy(dst[:, g * 8:(g + 1) * 8, t * 128:(t + 1) * 128],
                                  ptr[g][:, :].rearrange("p (c n) -> p c n", c=8))
                s_ev.inc(ins)

    sbase = s_sc.n

    @block.tensor
    def _(tensor):
        tensor.wait_ge(s_ev.h, 32)
        tensor.wait_ge(stW2.use[0].h, wq_mark0)
        tensor.wait_ge(stW2.use[1].h, wq_mark1)
        for t in range(16):
            if t >= 1:
                tensor.wait_ge(s_sc.h, sbase + (1 if t == 1 else 3 * (t - 1)))
            for i in range(16):
                lhsT = xqT[:, i * 2048 + t * 128: i * 2048 + (t + 1) * 128]
                for n in range(4):
                    ins = tensor.matmul(py[:, n * 512:(n + 1) * 512], lhsT,
                                        wqT[:, i * 2048 + n * 512: i * 2048 + (n + 1) * 512],
                                        start=(i == 0), stop=(i == 15))
            s_tc.inc(ins)
    ymm_base = s_tc.n - 16
    copy_marks, sq_marks, sqrt_marks = {}, {}, {}
    vb = s_vc.n
    yvar_marks = {j: (vb + 6 if j == 0 else vb + 9 * j + 3) for j in range(16)}

    @block.scalar
    def _(scalar):
        for t in range(17):
            if t < 16:
                scalar.wait_ge(s_tc.h, ymm_base + t + 1)
                if t >= 2:
                    scalar.wait_ge(s_st.h, 16 * (t - 1))
                if t >= 1:
                    scalar.wait_ge(s_sc.h, sq_marks[t - 1])
                ins_c = scalar.activation(out=yt[t % 2][:], in_=py[:, :],
                                          func=AF.Copy, accum_out=ysum[:, t:t + 1])
                m1 = s_sc.inc(ins_c)
                copy_marks[t] = m1
                scalar.wait_ge(s_sc.h, m1)
                sq_marks[t] = s_sc.inc(scalar.activation(
                    out=sqs[:], in_=yt[t % 2][:], func=AF.Square,
                    accum_out=ysq[:, t:t + 1]))
            if t >= 1:
                j = t - 1
                scalar.wait_ge(s_vc.h, yvar_marks[j])
                sqrt_marks[j] = s_sc.inc(scalar.activation(
                    out=yv2[:, j:j + 1], in_=yv1[:, j:j + 1],
                    func=AF.Sqrt, bias=epsc[:, 0:1]))

    @block.sync
    def _(sync):
        stX3.emit_loads(sync, xt, [xin[t * 128:(t + 1) * 128, :] for t in range(16)],
                        pre_waits=[(stX2.use[0].h, 8), (stX2.use[1].h, 8)])

    @block.vector
    def _(vector):
        for t in range(17):
            if t < 16:
                c = slice(t, t + 1)
                vector.wait_ge(s_sc.h, sq_marks[t])
                m1 = s_vc.inc(vector.tensor_mul(abcol[:, c], acol[:, 0:1], betac[:, c]))
                m1b = s_vc.inc(vector.tensor_scalar(ym[:, c], ysum[:, c], 1.0 / D,
                                                    None, ALU.mult))
                vector.wait_ge(s_vc.h, m1b)
                m2 = s_vc.inc(vector.tensor_mul(yv1[:, c], ym[:, c], ym[:, c]))
                vector.wait_ge(s_vc.h, m2)
                m3 = s_vc.inc(vector.scalar_tensor_tensor(yv1[:, c], ysq[:, c], 1.0 / D,
                                                          yv1[:, c], ALU.mult, ALU.subtract))
                m3b = s_vc.inc(vector.tensor_mul(yv2x[:, c], abcol[:, c], abcol[:, c]))
                vector.wait_ge(s_vc.h, max(m3, m3b))
                mv = s_vc.inc(vector.tensor_mul(yv1[:, c], yv1[:, c], yv2x[:, c]))
                assert mv == yvar_marks[t], (mv, yvar_marks[t])
            if t >= 1:
                j = t - 1
                c = slice(j, j + 1)
                vector.wait_ge(s_sc.h, sqrt_marks[j])
                m1 = s_vc.inc(vector.reciprocal(ysc[:, c], yv2[:, c]))
                vector.wait_ge(s_vc.h, m1)
                m2 = s_vc.inc(vector.tensor_mul(ysc[:, c], ysc[:, c], abcol[:, c]))
                vector.wait_ge(s_vc.h, m2)
                stX3.wait_tile(vector, j)
                if j >= 1:
                    vector.wait_ge(stX3.use[(j - 1) % 2].h, (j - 1) // 2 + 1)
                m3 = s_vc.inc(vector.tensor_scalar(vq[:], yt[j % 2][:], ym[:, c],
                                                   ysc[:, c], ALU.subtract, ALU.mult))
                vector.wait_ge(s_vc.h, m3)
                ins = vector.tensor_add(yt[j % 2][:], vq[:], xt[j % 2][:])
                stX3.done_tile(ins, j)

    @block.gpsimd
    def _(gpsimd):
        for t in range(16):
            gpsimd.wait_ge(stX3.use[t % 2].h, t // 2 + 1)
            gpsimd.dma_start(out=xout[t * 128:(t + 1) * 128, :], in_=yt[t % 2][:])\
                .then_inc(s_st.h, 16)
        s_st.n += 256
        gpsimd.wait_ge(s_st.h, s_st.n)

    return _bfin(p, standalone)


class Seq:

    def __init__(self, p, block, name="lock", fence=None):
        self.block = block
        self.s = p.sem(name)
        self.fence = fence

    def step(self, eng_name, emit, dma=False, extra_waits=()):
        target = self.s.n
        s = self.s
        fence = self.fence

        def body(eng):
            eng.wait_ge(s.h, target)
            for (h, v) in extra_waits:
                eng.wait_ge(h, v)
            r = emit(eng)
            if isinstance(r, list):
                fenced = (fence is not None and not dma
                          and eng_name in ("vector", "gpsimd", "scalar"))
                for ins in (r[:-1] if fenced else r):
                    s.inc(ins, 16 if dma else 1)
                if fenced:
                    if eng_name == "scalar":
                        s.inc(eng.copy(fence, fence))
                    else:
                        s.inc(eng.engine_nop())
            elif fence is not None and eng_name in ("vector", "gpsimd"):
                s.inc(eng.engine_nop())
            elif fence is not None and eng_name == "scalar":
                s.inc(eng.copy(fence, fence))
            else:
                s.inc(r, 16 if dma else 1)
        getattr(self.block, eng_name)(body)
        return self.s.n

    def mstep(self, eng_name, emit, extra_waits=()):
        return self.step(eng_name, emit, dma=False, extra_waits=extra_waits)


def build_l0_attn(stop_phase=99, p=None, block=None, xin=None, xout=None, pfx=""):
    standalone = p is None
    if standalone:
        p = Prog()
    nc = p.nc
    _bsetup(p, pfx)
    if xin is None:
        xin = p.din("xin", [2048, D])
    wqT = p.din("wqT", [D, 1024])
    wkT = p.din("wkT", [D, 2048])
    wvT = p.din("wvT", [D, 2048])
    wbT = p.din("wbT", [D, 16])
    waT = p.din("waT", [D, 8])
    wgT = p.din("wgT", [D, 1024])
    woT = p.din("woT", [1024, 2048])
    alog = p.din("alog", [1, 8])
    dtb = p.din("dtb", [8, 1])
    convq = p.din("convq", [128, 8 * 4])
    convk = p.din("convk", [128, 16 * 4])
    convv = p.din("convv", [128, 16 * 4])
    onw = p.din("onw", [1, 128])
    msT_in = p.din("msT_in", [128, 128])
    ms_in = p.din("ms_in", [128, 128])
    minc_in = p.din("minc_in", [128, 64])
    identb = p.din("identb", [128, 128], BF16)
    identf_in = p.din("identf_in", [128, 128], F32)
    if xout is None:
        xout = p.dout("xout", [2048, D])
    qc2 = p.dint("qc2", [1024, 2048], BF16)
    kc2 = p.dint("kc2", [2048, 2048], BF16)
    vc2 = p.dint("vc2", [2048, 2048], BF16)
    gg2 = p.dint("gg2", [2048, 1024], BF16)
    o2 = p.dint("o2", [1024, 2048], BF16)
    wo2 = p.dint("wo2", [1024, 2048], BF16)

    xnT = p.sbuf("xnT", [128, NT * 2048], BF16)
    prj = p.sbuf("prj", [128, 2052], F32)
    prbf = p.sbuf("prbf", [128, 2048], BF16)
    wblk = p.sbuf("wblk", [128, 128], F32)
    wblk2 = p.sbuf("wblk2", [128, 128], BF16)
    xt = [p.sbuf(f"xt{i}", [128, 2048], F32) for i in range(2)]
    xq = p.sbuf("xq", [128, 2048], BF16)
    vq = p.sbuf("vq", [128, 2048], F32)
    yt = p.sbuf("yt", [128, 2048], F32)
    cvw = p.sbuf("cvw", [128, 64], F32)
    rrow = p.sbuf("rrow", [1, 2048], F32)
    bst = p.sbuf("bst", [8, 4096], F32)
    grow_g = p.sbuf("grow_g", [8, 2048], F32)
    gcs = p.sbuf("gcs", [8, 2048], F32)
    negexp = p.sbuf("negexp", [8, 1], F32)
    dtbc = p.sbuf("dtbc", [8, 1], F32)
    onwb = p.sbuf("onwb", [128, 128], F32)
    msT = p.sbuf("msT", [128, 128], F32)
    ms = p.sbuf("ms", [128, 128], F32)
    minc = p.sbuf("minc", [128, 64], F32)
    ident = p.sbuf("ident", [128, 128], BF16)
    identf = p.sbuf("identf", [128, 128], F32)
    KT = p.sbuf("KT", [128, 1024], BF16)
    VT = p.sbuf("VT", [128, 1024], BF16)
    QT = p.sbuf("QT", [128, 512], BF16)
    Ktok = p.sbuf("Ktok", [128, 1024], BF16)
    oT8 = p.sbuf("oT8", [128, 512], BF16)
    Vtok = p.sbuf("Vtok", [128, 1024], BF16)
    BT = p.sbuf("BT", [128, 1024], BF16)
    Bm = p.sbuf("Bm", [128, 1024], BF16)
    P2 = p.sbuf("P2", [128, 1024], BF16)
    P2T = p.sbuf("P2T", [128, 1024], BF16)
    U0 = p.sbuf("U0", [128, 1024], BF16)
    U1 = p.sbuf("U1", [128, 1024], BF16)
    Gm = p.sbuf("Gm", [128, 512], BF16)
    S0 = p.sbuf("S0", [128, 1024], F32)
    S0b = p.sbuf("S0b", [128, 1024], BF16)
    csr = p.sbuf("csr", [1, 1024], F32)
    ncsr = p.sbuf("ncsr", [1, 1024], F32)
    bcol = p.sbuf("bcol", [128, 8], F32)
    wcol = p.sbuf("wcol", [128, 8], F32)
    btc = p.sbuf("btc", [128, 8], F32)
    bbc = p.sbuf("bbc", [128, 8], F32)
    cswin = p.sbuf("cswin", [64, 8], F32)
    btc2 = p.sbuf("btc2", [64, 8], F32)
    oq = p.sbuf("oq", [64, 1024], F32)
    osq = p.sbuf("osq", [64, 8], F32)
    ggb = p.sbuf("ggb", [64, 1024], BF16)
    epsc = p.sbuf("epsc", [128, 1], F32)
    onecol = p.sbuf("onecol", [128, 1], F32)
    ssq = p.sbuf("ssq", [128, 16], F32)
    ssum = p.sbuf("ssum", [128, 16], F32)
    mcol = p.sbuf("mcol", [128, 16], F32)
    rstd = p.sbuf("rstd", [128, 16], F32)
    vtmp = p.sbuf("vtmp", [128, 16], F32)
    vtmp2 = p.sbuf("vtmp2", [128, 16], F32)
    ysum = p.sbuf("ysum", [128, 16], F32)
    ysq = p.sbuf("ysq", [128, 16], F32)
    ym = p.sbuf("ym", [128, 16], F32)
    ysc = p.sbuf("ysc", [128, 16], F32)
    yc1 = p.sbuf("yc1", [128, 1], F32)
    yc2 = p.sbuf("yc2", [128, 1], F32)

    pt0 = p.psum("pt0", [128, 1024], BF16)
    pt1 = p.psum("pt1", [128, 1024], BF16)
    pA = p.psum("pA", [128, 2048], F32)
    pB = p.psum("pB", [128, 1024], F32)

    stX = Stream2(p, "stX", 16, n_consumers=2)
    stX2 = Stream2(p, "stX2", 16, n_consumers=1)
    s_vc = p.sem("s_vc")
    s_sc = p.sem("s_sc")
    s_tc = p.sem("s_tc")
    s_cst = p.sem("s_cst")
    s_slc = p.sem("s_slc")

    if block is None:
        block = p.es.enter_context(nc.Block())

    @block.sync
    def _(sync):
        for (dst, srcap) in [(ident, identb), (identf, identf_in), (msT, msT_in),
                             (ms, ms_in)]:
            sync.dma_start(out=dst[:], in_=srcap[:, :]).then_inc(s_cst.h, 16)
        sync.dma_start(out=minc[:], in_=minc_in[:, :]).then_inc(s_cst.h, 16)
        sync.dma_start(out=cvw[:, 0:32], in_=convq[:, :]).then_inc(s_cst.h, 16)
        sync.dma_start(out=rrow[0:1, 0:8], in_=alog[:, :]).then_inc(s_cst.h, 16)
        sync.dma_start(out=dtbc[:], in_=dtb[:, :]).then_inc(s_cst.h, 16)
        sync.dma_start(out=rrow[0:1, 8:136], in_=onw[:, :]).then_inc(s_cst.h, 16)
    s_cst.n += 9 * 16

    @block.vector
    def _(vector):
        s_vc.inc(vector.memset(epsc[:], 1e-5))
        s_vc.inc(vector.memset(onecol[:], 1.0))
        s_vc.inc(vector.memset(prj[:, 0:3], 0.0))
        s_vc.inc(vector.memset(S0[:, :], 0.0))
        s_vc.inc(vector.memset(S0b[:, :], 0.0))
    cst_mark = s_vc.n

    ln_mark = emit_ln_stats(p, block, xin, xt, stX, pA[:, :], ssq, ssum,
                            mcol, rstd, vtmp, vtmp2, s_sc, s_vc, epsc[:, 0:1])

    @block.sync
    def _(sync):
        stX2.emit_loads(sync, xt, [xin[t * 128:(t + 1) * 128, :] for t in range(16)],
                        pre_waits=[(stX.use[0].h, 16), (stX.use[1].h, 16)])

    xnt_tr0 = s_tc.n

    @block.vector
    def _(vector):
        for t in range(16):
            stX2.wait_tile(vector, t)
            vector.wait_ge(s_vc.h, ln_mark)
            if t >= 1:
                vector.wait_ge(stX2.use[(t - 1) % 2].h, (t - 1) // 2 + 1)
                vector.wait_ge(s_tc.h, xnt_tr0 + 2 * t)
            ins = vector.tensor_scalar(xq[:], xt[t % 2][:], mcol[:, t:t + 1],
                                       rstd[:, t:t + 1], ALU.subtract, ALU.mult)
            stX2.done_tile(ins, t)

    @block.tensor
    def _(tensor):
        tensor.wait_ge(s_cst.h, s_cst.n)
        for t in range(16):
            tensor.wait_ge(stX2.use[t % 2].h, t // 2 + 1)
            for g in range(2):
                ptg = [pt0, pt1][g]
                if t >= 1:
                    tensor.wait_ge(s_slc.h, 2 * (t - 1) + g + 1)
                for j in range(8):
                    ch = g * 8 + j
                    ins = tensor.transpose(ptg[:, j * 128:(j + 1) * 128],
                                           xq[:, ch * 128:(ch + 1) * 128], ident[:])
                s_tc.inc(ins)

    @block.scalar
    def _(scalar):
        for t in range(16):
            for g in range(2):
                scalar.wait_ge(s_tc.h, xnt_tr0 + 2 * t + g + 1)
                dst = xnT[:, :].rearrange("pp (c n) -> pp c n", c=16)
                ins = scalar.copy(dst[:, g * 8:(g + 1) * 8, t * 128:(t + 1) * 128],
                                  [pt0, pt1][g][:, :].rearrange("pp (c n) -> pp c n",
                                                                c=8))
                s_slc.inc(ins)
    xnT_done = s_slc.n

    fencec = p.sbuf("fencec", [1, 1], F32)
    seq = Seq(p, block, "lock", fence=fencec[0:1, 0:1])
    seq.step("vector", lambda v: v.engine_nop(),
             extra_waits=[(s_slc.h, xnT_done), (s_vc.h, cst_mark)])

    cvk = p.sbuf("cvk", [128, 64], F32)
    cvv = p.sbuf("cvv", [128, 64], F32)
    seq.step("sync", lambda s: [
        s.dma_start(out=cvk[:, :], in_=convk[:, :]),
        s.dma_start(out=cvv[:, :], in_=convv[:, :])], dma=True)

    seq.step("tensor", lambda t: t.matmul(
        pB[:, 0:128], onecol[0:1, :].to_broadcast([1, 128]),
        rrow[0:1, 8:136], start=True, stop=True))
    seq.step("vector", lambda v: v.tensor_copy(onwb[:, :], pB[:, 0:128]))
    seq.step("tensor", lambda t: t.transpose(pA[:, 0:1][0:8, 0:1],
                                             rrow[0:1, 0:8], identf[0:1, 0:1]))
    seq.step("scalar", lambda s: s.activation(out=negexp[:], in_=pA[0:8, 0:1],
                                              func=AF.Exp))
    seq.step("vector", lambda v: v.tensor_scalar(negexp[:], negexp[:], -1.0,
                                                 None, ALU.mult))

    def proj_rows(wT_ap, out2, nblk, conv_tbl, kind):
        ncols = wT_ap.shape[1]
        for jb in range(nblk):
            wv = wT_ap.rearrange("(dc pp) j -> pp dc j", pp=128)
            seq.step("sync", lambda s, jb=jb, wv=wv: [s.dma_start(
                out=vq[:, :].rearrange("pp (dc j) -> pp dc j", dc=16),
                in_=wv[:, :, jb * 128:(jb + 1) * 128])], dma=True)
            seq.step("vector", lambda v: v.tensor_copy(prbf[:, :], vq[:, :]))
            seq.step("tensor", lambda t: [
                t.matmul(pA[:, tc * 512:(tc + 1) * 512],
                         prbf[:, d * 128:(d + 1) * 128],
                         xnT[:, d * 2048 + tc * 512: d * 2048 + (tc + 1) * 512],
                         start=(d == 0), stop=(d == 15))
                for d in range(16) for tc in range(4)][-1])
            seq.step("scalar", lambda s: s.copy(prj[:, 3:2051][:, 0:2048], pA[:, :]))
            if conv_tbl is not None:
                seq.step("vector", lambda v, jb=jb: v.tensor_scalar(
                    yt[:, :], prj[:, 0:2048], conv_tbl[:, jb * 4:jb * 4 + 1],
                    None, ALU.mult))
                for j in range(1, 4):
                    seq.step("vector", lambda v, j=j, jb=jb: v.scalar_tensor_tensor(
                        yt[:, :], prj[:, j:j + 2048], conv_tbl[:, jb * 4 + j:
                                                                jb * 4 + j + 1],
                        yt[:, :], ALU.mult, ALU.add))
                seq.step("scalar", lambda s: s.activation(out=yt[:, :],
                                                          in_=yt[:, :],
                                                          func=AF.Silu))
            else:
                seq.step("scalar", lambda s: s.copy(yt[:, :], prj[:, 3:2051]))
            if kind in ("q", "k"):
                seq.step("scalar", lambda s: s.activation(out=vq[:, :],
                                                          in_=yt[:, :],
                                                          func=AF.Square))
                seq.step("vector", lambda v: v.tensor_copy(prbf[:, :], vq[:, :]))
                seq.step("tensor", lambda t: [
                    t.matmul(pA[0:1, tc * 512:(tc + 1) * 512],
                             onecol[:, 0:1], vq[:, tc * 512:(tc + 1) * 512],
                             start=True, stop=True) for tc in range(4)][-1])
                sc = 128.0 if kind == "q" else 1.0
                seq.step("scalar", lambda s, sc=sc: s.activation(
                    out=rrow[0:1, 0:2048], in_=pA[0:1, 0:2048],
                    func=AF.Sqrt, scale=sc))
                seq.step("vector", lambda v: v.tensor_scalar(
                    rrow[0:1, :], rrow[0:1, :], 1e-12 * (11.3137085 if kind == "q"
                                                         else 1.0),
                    None, ALU.max))
                seq.step("vector", lambda v: v.reciprocal(rrow[0:1, :],
                                                          rrow[0:1, :]))
                seq.step("tensor", lambda t: [
                    t.matmul(pB[:, tc * 512:(tc + 1) * 512],
                             onecol[0:1, :].to_broadcast([1, 128]),
                             rrow[0:1, tc * 512:(tc + 1) * 512],
                             start=True, stop=True) for tc in range(2)][-1])
                seq.step("vector", lambda v: v.tensor_mul(
                    prbf[:, 0:1024], yt[:, 0:1024], pB[:, 0:1024]))
                seq.step("tensor", lambda t: [
                    t.matmul(pB[:, (tc - 2) * 512:(tc - 1) * 512],
                             onecol[0:1, :].to_broadcast([1, 128]),
                             rrow[0:1, tc * 512:(tc + 1) * 512],
                             start=True, stop=True) for tc in range(2, 4)][-1])
                seq.step("vector", lambda v: v.tensor_mul(
                    prbf[:, 1024:2048], yt[:, 1024:2048], pB[:, 0:1024]))
            else:
                if kind == "g":
                    seq.step("scalar", lambda s: s.activation(
                        out=yt[:, :], in_=yt[:, :], func=AF.Silu))
                seq.step("vector", lambda v: v.tensor_copy(prbf[:, :], yt[:, :]))
            if kind != "g":
                seq.step("sync", lambda s, jb=jb: [s.dma_start(
                    out=out2[jb * 128:(jb + 1) * 128, :], in_=prbf[:, :])], dma=True)
            if kind == "g" and stop_phase != 175:
                pass
            if kind == "g" and stop_phase != 175:
                for g in range(2):
                    seq.step("tensor", lambda t, g=g: [
                        t.transpose([pt0, pt1][g][:, j * 128:(j + 1) * 128],
                                    prbf[:, (g * 8 + j) * 128:
                                         (g * 8 + j + 1) * 128], ident[:])
                        for j in range(8)][-1])
                    seq.step("vector", lambda v, g=g: v.tensor_copy(
                        xq[:, g * 1024:(g + 1) * 1024], [pt0, pt1][g][:, :]))
                if stop_phase != 176:
                    for tt0 in range(0, 16, 4):
                        seq.step("sync", lambda s, jb=jb, tt0=tt0: [s.dma_start(
                            out=gg2[tt * 128:(tt + 1) * 128,
                                    jb * 128:(jb + 1) * 128],
                            in_=xq[:, tt * 128:(tt + 1) * 128])
                            for tt in range(tt0, tt0 + 4)], dma=True)

    if stop_phase <= 1:
        seq.step("sync", lambda s: [s.dma_start(out=xout[0:128, :],
                                                in_=yt[:, :])], dma=True)
        return _bfin(p, standalone)
    def _stop():
        seq.step("sync", lambda s: [s.dma_start(out=xout[0:128, :],
                                                in_=yt[:, :])], dma=True)
        return _bfin(p, standalone)

    proj_rows(wqT, qc2, 8, cvw, "q")
    if stop_phase == 15:
        return _stop()
    proj_rows(wkT, kc2, 16, cvk, "k")
    if stop_phase == 16:
        return _stop()
    proj_rows(wvT, vc2, 16, cvv, "v")
    if stop_phase == 17:
        return _stop()
    proj_rows(wgT, gg2, 8, None, "g")
    if stop_phase in (18, 175, 176):
        return _stop()

    for pp in range(2):
        for d in range(16):
            seq.step("sync", lambda s, d=d, pp=pp: [s.dma_start(
                out=wblk[:, 0:8], in_=wbT[d * 128:(d + 1) * 128,
                                          pp * 8:(pp + 1) * 8])], dma=True)
            seq.step("vector", lambda v: v.tensor_copy(wblk2[:, 0:8],
                                                       wblk[:, 0:8]))
            seq.step("tensor", lambda t, d=d: [
                t.matmul(pA[0:8, tc * 512:(tc + 1) * 512], wblk2[:, 0:8],
                         xnT[:, d * 2048 + tc * 512: d * 2048 + (tc + 1) * 512],
                         start=(d == 0), stop=(d == 15)) for tc in range(4)][-1])
        seq.step("scalar", lambda s, pp=pp: s.activation(
            out=bst[:, pp * 2048:(pp + 1) * 2048], in_=pA[0:8, :],
            func=AF.Sigmoid))
    seq.step("vector", lambda v: v.tensor_scalar(bst[:, :], bst[:, :], 2.0,
                                                 None, ALU.mult))
    if stop_phase == 19:
        return _stop()

    for d in range(16):
        seq.step("sync", lambda s, d=d: [s.dma_start(
            out=wblk[:, 0:8], in_=waT[d * 128:(d + 1) * 128, :])], dma=True)
        seq.step("vector", lambda v: v.tensor_copy(wblk2[:, 0:8], wblk[:, 0:8]))
        seq.step("tensor", lambda t, d=d: [
            t.matmul(pA[0:8, tc * 512:(tc + 1) * 512], wblk2[:, 0:8],
                     xnT[:, d * 2048 + tc * 512: d * 2048 + (tc + 1) * 512],
                     start=(d == 0), stop=(d == 15)) for tc in range(4)][-1])
    seq.step("vector", lambda v: v.tensor_scalar(grow_g[:, :], pA[0:8, :],
                                                 dtbc[0:8, 0:1], None, ALU.add))
    seq.step("scalar", lambda s: s.activation(out=yt[0:8, 0:2048],
                                              in_=grow_g[:, :], func=AF.Abs))
    seq.step("scalar", lambda s: s.activation(out=yt[0:8, 0:2048],
                                              in_=yt[0:8, 0:2048],
                                              func=AF.Exp, scale=-1.0))
    seq.step("scalar", lambda s: s.activation(out=yt[0:8, 0:2048],
                                              in_=yt[0:8, 0:2048],
                                              func=AF.Ln, bias=onecol[0:8, 0:1]))
    seq.step("vector", lambda v: v.scalar_tensor_tensor(
        grow_g[:, :], grow_g[:, :], 0.0, yt[0:8, 0:2048], ALU.max, ALU.add))
    seq.step("vector", lambda v: v.tensor_scalar(grow_g[:, :], grow_g[:, :],
                                                 negexp[0:8, 0:1], None, ALU.mult))
    seq.step("vector", lambda v: v.memset(yt[0:8, 0:2048], 1.0))
    seq.step("vector", lambda v: v.tensor_tensor_scan(
        gcs[:, 0:2048], yt[0:8, 0:2048], grow_g[:, :],
        0.0, ALU.mult, ALU.add))
    for ch in range(8):
        seq.step("sync", lambda s, ch=ch: [s.dma_start(
            out=vq[:, :], in_=woT[ch * 128:(ch + 1) * 128, :])], dma=True)
        seq.step("vector", lambda v, ch=ch: v.tensor_copy(
            xnT[:, ch * 2048:(ch + 1) * 2048], vq[:, :]))



    brow1 = p.sbuf("brow1", [1, 1024], F32)
    browo = p.sbuf("browo", [1, 512], F32)
    prow = p.sbuf("prow", [1, 8], F32)
    erow = p.sbuf("erow", [1, 8], F32)
    zcol8 = p.sbuf("zcol8", [8, 1], F32)
    endc = p.sbuf("endc", [8, 1], F32)
    cscol = p.sbuf("cscol", [128, 8], F32)
    bCrow = p.sbuf("bCrow", [1, 8], F32)
    bCcol = p.sbuf("bCcol", [128, 8], F32)

    onesr_sb = p.sbuf("onesr_sb", [1, 128], F32)
    seq.step("vector", lambda v: v.memset(onesr_sb[:, :], 1.0))
    ones_r = onesr_sb[0:1, 0:128]
    ones_r64 = onesr_sb[0:1, 0:64]

    seq.step("vector", lambda v: v.memset(zcol8[:], 0.0))

    if stop_phase <= 2:
        seq.step("sync", lambda s: [s.dma_start(out=xout[0:128, :],
                                                in_=yt[:, :])], dma=True)
        return _bfin(p, standalone)
    for c in range(32 if stop_phase > 3 else 1):
        w0 = 64 * c

        def bwin2(h):
            return bst[h:h + 1, :].rearrange("o (pp n) -> o pp n", pp=2)[
                :, :, w0:w0 + 64]

        seq.mstep("tensor", lambda t: [
            t.transpose(pB[0:64, 0:8], gcs[0:8, w0:w0 + 64], identf[0:8, 0:8]),
            t.transpose(pB[0:1, 24:32],
                        gcs[0:8, w0 - 1:w0] if c > 0 else zcol8[0:8, 0:1],
                        identf[0:8, 0:8]),
            t.transpose(pB[0:1, 32:40], gcs[0:8, w0 + 63:w0 + 64],
                        identf[0:8, 0:8]),
            t.transpose(pB[0:64, 8:16], bst[0:8, w0:w0 + 64],
                        identf[0:8, 0:8]),
            t.transpose(pB[0:64, 16:24], bst[0:8, 2048 + w0:2048 + w0 + 64],
                        identf[0:8, 0:8])])
        seq.mstep("vector", lambda v: [
            v.tensor_copy(cswin[:, :], pB[0:64, 0:8]),
            v.tensor_copy(cscol[0:64, :], pB[0:64, 0:8]),
            v.tensor_copy(cscol[64:128, :], pB[0:64, 0:8]),
            v.tensor_copy(prow[:], pB[0:1, 24:32]),
            v.tensor_copy(erow[:], pB[0:1, 32:40]),
            v.tensor_copy(btc[0:64, :], pB[0:64, 8:16]),
            v.tensor_copy(btc[64:128, :], pB[0:64, 16:24]),
            v.tensor_copy(btc2[:, :], pB[0:64, 16:24])])
        if stop_phase == 41 and c == 0:
            seq.step("sync", lambda s: [s.dma_start(out=xout[0:128, :],
                                                    in_=yt[:, :])], dma=True)
            return _bfin(p, standalone)
        seq.mstep("tensor", lambda t: [
            t.transpose(pA[0:1, h * 64:(h + 1) * 64], cswin[0:64, h:h + 1],
                        identf[0:64, 0:64]) for h in range(8)])
        seq.mstep("vector", lambda v: [
            v.tensor_copy(csr[0:1, h * 128 + 64 * z:h * 128 + 64 * (z + 1)],
                          pA[0:1, h * 64:(h + 1) * 64])
            for h in range(8) for z in range(2)] + [
            v.tensor_scalar(ncsr[0:1, h * 128 + 64 * z:h * 128 + 64 * (z + 1)],
                            pA[0:1, h * 64:(h + 1) * 64], -1.0, None, ALU.mult)
            for h in range(8) for z in range(2)])
        if stop_phase == 42 and c == 0:
            seq.step("sync", lambda s: [s.dma_start(out=xout[0:128, :],
                                                    in_=yt[:, :])], dma=True)
            return _bfin(p, standalone)

        if stop_phase == 43 and c == 0:
            seq.step("sync", lambda s: [s.dma_start(out=xout[0:128, :],
                                                    in_=yt[:, :])], dma=True)
            return _bfin(p, standalone)
        seq.mstep("tensor", lambda t: [
            t.matmul(pB[:, 8:16], ones_r, prow[:, :], start=True, stop=True),
            t.matmul(pB[:, 512:520], ones_r, erow[:, :], start=True, stop=True)])
        seq.mstep("vector", lambda v: [
            v.tensor_copy(bcol[:, :], pB[:, 8:16]),
            v.tensor_copy(wcol[:, :], pB[:, 512:520])])
        if stop_phase == 431 and c == 0:
            seq.step("sync", lambda s: [s.dma_start(out=xout[0:128, :],
                                                    in_=yt[:, :])], dma=True)
            return _bfin(p, standalone)
        seq.mstep("vector", lambda v: [
            v.tensor_sub(bCcol[:, :], wcol[:, :], bcol[:, :]),
            v.tensor_sub(bcol[:, :], cscol[:, :], bcol[:, :]),
            v.tensor_sub(wcol[:, :], wcol[:, :], cscol[:, :])])
        seq.mstep("scalar", lambda s: [
            s.activation(out=bCcol[:, :], in_=bCcol[:, :], func=AF.Exp),
            s.activation(out=bcol[:, :], in_=bcol[:, :], func=AF.Exp),
            s.activation(out=wcol[:, :], in_=wcol[:, :], func=AF.Exp)])
        if stop_phase == 44 and c == 0:
            seq.step("sync", lambda s: [s.dma_start(out=xout[0:128, :],
                                                    in_=yt[:, :])], dma=True)
            return _bfin(p, standalone)
        if stop_phase == 441 and c == 0:
            seq.step("sync", lambda s: [s.dma_start(out=xout[0:128, :],
                                                    in_=yt[:, :])], dma=True)
            return _bfin(p, standalone)
        seq.mstep("tensor", lambda t: [x for h in range(8) for x in (
            t.transpose(pA[0:1, h * 128:h * 128 + 64], btc[0:64, h:h + 1],
                        identf[0:64, 0:64]),
            t.transpose(pA[0:1, h * 128 + 64:h * 128 + 128],
                        btc2[0:64, h:h + 1], identf[0:64, 0:64]))])
        if stop_phase == 442 and c == 0:
            seq.step("sync", lambda s: [s.dma_start(out=xout[0:128, :],
                                                    in_=yt[:, :])], dma=True)
            return _bfin(p, standalone)
        seq.mstep("vector", lambda v: [
            v.tensor_scalar(brow1[0:1, :], pA[0:1, 0:1024], -1.0, None,
                            ALU.mult)] + [
            v.scalar_tensor_tensor(bbc[:, h:h + 1], btc[:, h:h + 1], -1.0,
                                   bcol[:, h:h + 1], ALU.mult, ALU.mult)
            for h in range(8)])
        if stop_phase == 31:
            seq.step("sync", lambda s: [s.dma_start(out=xout[0:128, :],
                                                    in_=yt[:, :])], dma=True)
            return _bfin(p, standalone)
        def _ld_kv(s, src, dst):
            sv = src.rearrange("(z h dd) t -> z dd h t", z=2, h=8)
            dv = dst[:, 0:1024].rearrange("dd (h z j) -> dd h z j", h=8, z=2)
            return [s.dma_start(out=dv[:, :, z, :], in_=sv[z][:, :, w0:w0 + 64])
                    for z in range(2)]
        seq.step("sync", lambda s: _ld_kv(s, kc2, KT) + _ld_kv(s, vc2, VT) + [
            s.dma_start(
                out=QT[:, 0:512].rearrange("dd (h j) -> dd h j", h=8),
                in_=qc2.rearrange("(h dd) t -> dd h t", h=8)[:, :, w0:w0 + 64]),
            s.dma_start(out=ggb[:, :], in_=gg2[w0:w0 + 64, :])],
            dma=True)
        if stop_phase == 32:
            seq.step("sync", lambda s: [s.dma_start(out=xout[0:128, :],
                                                    in_=yt[:, :])], dma=True)
            return _bfin(p, standalone)
        seq.mstep("tensor", lambda t: [
            t.matmul(pA[:, h * 128:(h + 1) * 128], KT[:, h * 128:(h + 1) * 128],
                     KT[:, h * 128:(h + 1) * 128], start=True, stop=True)
            for h in range(8)])
        if stop_phase == 33:
            seq.step("sync", lambda s: [s.dma_start(out=xout[0:128, :],
                                                    in_=yt[:, :])], dma=True)
            return _bfin(p, standalone)
        seq.mstep("tensor", lambda t: [x for h in range(8) for x in (
            t.matmul(pA[:, 1024 + h * 128:1024 + (h + 1) * 128],
                     ncsr[0:1, h * 128:(h + 1) * 128], ones_r,
                     start=True, stop=False),
            t.matmul(pA[:, 1024 + h * 128:1024 + (h + 1) * 128], ones_r,
                     csr[0:1, h * 128:(h + 1) * 128], start=False, stop=False),
            t.matmul(pA[:, 1024 + h * 128:1024 + (h + 1) * 128], identf[:, :],
                     msT[:, :], start=False, stop=True))])
        seq.step("scalar", lambda s: s.activation(out=prj[:, 0:1024],
                                                  in_=pA[:, 1024:2048],
                                                  func=AF.Exp))
        seq.step("tensor", lambda t: [
            t.matmul(pB[:, n * 512:(n + 1) * 512], ones_r,
                     brow1[0:1, n * 512:(n + 1) * 512], start=True, stop=True)
            for n in range(2)][-1])
        seq.step("vector", lambda v: v.tensor_mul(prj[:, 0:1024], prj[:, 0:1024],
                                                  pA[:, 0:1024]))
        seq.step("vector", lambda v: v.tensor_mul(BT[:, :], prj[:, 0:1024],
                                                  pB[:, :]))
        if stop_phase == 34:
            seq.step("sync", lambda s: [s.dma_start(out=xout[0:128, :],
                                                    in_=yt[:, :])], dma=True)
            return _bfin(p, standalone)
        seq.mstep("tensor", lambda t: [
            t.transpose([pt0, pt1][h // 4][:, (h % 4) * 256:(h % 4) * 256 + 128],
                        BT[:, h * 128:(h + 1) * 128], ident[:]) for h in range(8)])
        seq.mstep("vector", lambda v: [
            v.tensor_copy(
                Bm[:, 0:512].rearrange("pp (a b) -> pp a b", a=4),
                pt0[:, :].rearrange("pp (a b) -> pp a b", a=4)[:, :, 0:128]),
            v.tensor_copy(
                Bm[:, 512:1024].rearrange("pp (a b) -> pp a b", a=4),
                pt1[:, :].rearrange("pp (a b) -> pp a b", a=4)[:, :, 0:128])])
        seq.mstep("tensor", lambda t: [
            t.matmul(pA[:, h * 128:(h + 1) * 128], KT[:, h * 128:(h + 1) * 128],
                     S0b[:, h * 128:(h + 1) * 128], start=True, stop=True)
            for h in range(8)] + [
            t.transpose([pt0, pt1][h // 4][:, (h % 4) * 256:(h % 4) * 256 + 128],
                        VT[:, h * 128:(h + 1) * 128], ident[:]) for h in range(8)])
        seq.mstep("vector", lambda v: [
            v.tensor_scalar(Vtok[:, h * 128:(h + 1) * 128],
                            [pt0, pt1][h // 4][:, (h % 4) * 256:(h % 4) * 256 + 128],
                            btc[:, h:h + 1], None, ALU.mult) for h in range(8)])
        seq.mstep("vector", lambda v: [
            v.scalar_tensor_tensor(U0[:, h * 128:(h + 1) * 128],
                                   pA[:, h * 128:(h + 1) * 128],
                                   bbc[:, h:h + 1], Vtok[:, h * 128:(h + 1) * 128],
                                   ALU.mult, ALU.add) for h in range(8)])
        if stop_phase == 36:
            seq.step("sync", lambda s: [s.dma_start(out=xout[0:128, :],
                                                    in_=yt[:, :])], dma=True)
            return _bfin(p, standalone)
        Ubufs = [U0, U1]
        for k in range(7):
            lhs = BT if k == 0 else P2T
            sq_l = (BT, Bm) if k == 0 else (P2T, P2)
            if k < 6:
                seq.mstep("tensor", lambda t, lhs=lhs, sq_l=sq_l, k=k: [
                    t.matmul(pB[:, h * 128:(h + 1) * 128],
                             lhs[:, h * 128:(h + 1) * 128],
                             Ubufs[k % 2][:, h * 128:(h + 1) * 128],
                             start=True, stop=True) for h in range(8)] + [
                    t.matmul(pA[:, h * 128:(h + 1) * 128],
                             sq_l[0][:, h * 128:(h + 1) * 128],
                             sq_l[1][:, h * 128:(h + 1) * 128],
                             start=True, stop=True) for h in range(8)] + [
                    t.matmul(pA[:, 1024 + h * 128:1024 + (h + 1) * 128],
                             sq_l[1][:, h * 128:(h + 1) * 128],
                             sq_l[0][:, h * 128:(h + 1) * 128],
                             start=True, stop=True) for h in range(8)])
                seq.mstep("vector", lambda v, k=k: [
                    v.tensor_add(Ubufs[(k + 1) % 2][:, :], pB[:, :],
                                 Ubufs[k % 2][:, :]),
                    v.tensor_copy(P2[:, :], pA[:, 0:1024]),
                    v.tensor_copy(P2T[:, :], pA[:, 1024:2048])])
            else:
                seq.mstep("tensor", lambda t, lhs=lhs, k=k: [
                    t.matmul(pB[:, h * 128:(h + 1) * 128],
                             lhs[:, h * 128:(h + 1) * 128],
                             Ubufs[k % 2][:, h * 128:(h + 1) * 128],
                             start=True, stop=True) for h in range(8)])
                seq.step("vector", lambda v, k=k: v.tensor_add(
                    Ubufs[(k + 1) % 2][:, :], pB[:, :], Ubufs[k % 2][:, :]))
        Ufin = Ubufs[7 % 2]
        if stop_phase == 37:
            seq.step("sync", lambda s: [s.dma_start(out=xout[0:128, :],
                                                    in_=yt[:, :])], dma=True)
            return _bfin(p, standalone)
        seq.mstep("vector", lambda v: [
            v.tensor_scalar(browo[0:1, h * 64:(h + 1) * 64],
                            csr[0:1, h * 128:h * 128 + 64],
                            prow[0:1, h:h + 1], None, ALU.subtract)
            for h in range(8)])
        seq.mstep("tensor", lambda t: [
            t.matmul(pA[:, 1024 + h * 64:1024 + (h + 1) * 64],
                     KT[:, h * 128:(h + 1) * 128], QT[:, h * 64:(h + 1) * 64],
                     start=True, stop=True) for h in range(8)] + [
            x for h in range(8) for x in (
            t.matmul(pA[:, h * 64:(h + 1) * 64],
                     ncsr[0:1, h * 128:(h + 1) * 128], ones_r64,
                     start=True, stop=False),
            t.matmul(pA[:, h * 64:(h + 1) * 64], ones_r,
                     csr[0:1, h * 128:h * 128 + 64], start=False, stop=False),
            t.matmul(pA[:, h * 64:(h + 1) * 64], identf[:, :], minc[:, :],
                     start=False, stop=True))])
        seq.mstep("scalar", lambda s: [
            s.activation(out=yt[:, 0:512], in_=pA[:, 0:512], func=AF.Exp),
            s.activation(out=browo[:, :], in_=browo[:, :], func=AF.Exp)])
        seq.step("vector", lambda v: v.tensor_mul(Gm[:, :], yt[:, 0:512],
                                                  pA[:, 1024:1536]))
        seq.step("tensor", lambda t: t.matmul(pB[:, 0:512], ones_r, browo[:, :],
                                              start=True, stop=True))
        seq.mstep("vector", lambda v: [
            v.tensor_mul(QT[:, :], QT[:, :], pB[:, 0:512])][0:1])
        seq.mstep("tensor", lambda t: [x for h in range(8) for x in (
            t.matmul(pB[0:64, h * 128:(h + 1) * 128],
                     Gm[:, h * 64:(h + 1) * 64],
                     Ufin[:, h * 128:(h + 1) * 128], start=True, stop=False),
            t.matmul(pB[0:64, h * 128:(h + 1) * 128],
                     QT[:, h * 64:(h + 1) * 64],
                     S0b[:, h * 128:(h + 1) * 128], start=False, stop=True))])
        if stop_phase == 38:
            seq.step("sync", lambda s: [s.dma_start(out=xout[0:128, :],
                                                    in_=yt[:, :])], dma=True)
            return _bfin(p, standalone)
        seq.step("scalar", lambda s: s.activation(out=yt[0:64, 0:1024],
                                                  in_=pB[0:64, :], func=AF.Square))
        seq.step("vector", lambda v: v.reduce_sum(
            osq[:, :], yt[0:64, 0:1024].rearrange("pp (hh e) -> pp hh e", hh=8),
            axis=AX.X))
        seq.step("scalar", lambda s: s.activation(out=osq[:, :], in_=osq[:, :],
                                                  func=AF.Sqrt, scale=1.0 / 128.0,
                                                  bias=epsc[0:64, 0:1]))
        seq.step("vector", lambda v: v.reciprocal(osq[:, :], osq[:, :]))
        seq.mstep("vector", lambda v: [
            v.scalar_tensor_tensor(oq[:, h * 128:(h + 1) * 128],
                                   pB[0:64, h * 128:(h + 1) * 128],
                                   osq[:, h:h + 1], onwb[0:64, :],
                                   ALU.mult, ALU.mult) for h in range(8)])
        seq.step("vector", lambda v: v.tensor_mul(xq[0:64, 0:1024], oq[:, :], ggb[:, :]))
        seq.mstep("tensor", lambda t: [
            t.transpose([pt0, pt1][h // 4][:, (h % 4) * 256 + 128:(h % 4) * 256 + 192],
                        xq[0:64, h * 128:(h + 1) * 128], ident[0:64, 0:64])
            for h in range(8)] + [
            t.transpose([pt0, pt1][h // 4][:, (h % 4) * 256:(h % 4) * 256 + 128],
                        KT[:, h * 128:(h + 1) * 128], ident[:]) for h in range(8)])
        seq.mstep("vector", lambda v: [
            v.tensor_copy(oT8[:, h * 64:(h + 1) * 64],
                          [pt0, pt1][h // 4][:, (h % 4) * 256 + 128:(h % 4) * 256 + 192])
            for h in range(8)] + [
            v.tensor_scalar(Ktok[:, h * 128:(h + 1) * 128],
                            [pt0, pt1][h // 4][:, (h % 4) * 256:(h % 4) * 256 + 128],
                            wcol[:, h:h + 1], None, ALU.mult) for h in range(8)])
        seq.step("sync", lambda s: [s.dma_start(
            out=o2[h * 128:(h + 1) * 128, w0:w0 + 64],
            in_=oT8[:, h * 64:(h + 1) * 64]) for h in range(8)], dma=True)
        if stop_phase == 39:
            seq.step("sync", lambda s: [s.dma_start(out=xout[0:128, :],
                                                    in_=yt[:, :])], dma=True)
            return _bfin(p, standalone)
        seq.mstep("tensor", lambda t: [
            t.matmul(pA[:, h * 128:(h + 1) * 128], Ktok[:, h * 128:(h + 1) * 128],
                     Ufin[:, h * 128:(h + 1) * 128], start=True, stop=True)
            for h in range(8)])
        seq.mstep("vector", lambda v: [
            v.scalar_tensor_tensor(S0[:, h * 128:(h + 1) * 128],
                                   S0[:, h * 128:(h + 1) * 128],
                                   bCcol[:, h:h + 1],
                                   pA[:, h * 128:(h + 1) * 128],
                                   ALU.mult, ALU.add) for h in range(8)])
        seq.step("vector", lambda v: v.tensor_copy(S0b[:, :], S0[:, :]))

    if stop_phase <= 4:
        seq.step("sync", lambda s: [s.dma_start(out=xout[0:128, :],
                                                in_=yt[:, :])], dma=True)
        return _bfin(p, standalone)
    ov3 = o2.rearrange("(h pp) tt -> pp h tt", pp=128)
    for t in range(16):
        seq.step("sync", lambda s, t=t: [
            s.dma_start(out=xt[0][:, :], in_=xin[t * 128:(t + 1) * 128, :]),
            s.dma_start(out=xq[:, 0:1024].rearrange("pp (h j) -> pp h j", h=8),
                        in_=ov3[:, :, t * 128:(t + 1) * 128])], dma=True)
        seq.step("tensor", lambda te: [
            te.matmul(pA[:, n * 512:(n + 1) * 512], xq[:, h * 128:(h + 1) * 128],
                      xnT[:, h * 2048 + n * 512:h * 2048 + (n + 1) * 512],
                      start=(h == 0), stop=(h == 7))
            for h in range(8) for n in range(4)][-1])
        seq.step("scalar", lambda s, t=t: s.activation(
            out=yt[:, :], in_=pA[:, :], func=AF.Copy, accum_out=ysum[:, t:t + 1]))
        seq.step("scalar", lambda s, t=t: s.activation(
            out=vq[:, :], in_=yt[:, :], func=AF.Square, accum_out=ysq[:, t:t + 1]))
        cc = slice(t, t + 1)
        seq.step("vector", lambda v, cc=cc: v.tensor_scalar(
            ym[:, cc], ysum[:, cc], 1.0 / D, None, ALU.mult))
        seq.step("vector", lambda v, cc=cc: v.tensor_mul(yc1[:], ym[:, cc],
                                                         ym[:, cc]))
        seq.step("vector", lambda v, cc=cc: v.scalar_tensor_tensor(
            yc1[:], ysq[:, cc], 1.0 / D, yc1[:], ALU.mult, ALU.subtract))
        seq.step("scalar", lambda s: s.activation(out=yc2[:], in_=yc1[:],
                                                  func=AF.Sqrt, bias=epsc[:, 0:1]))
        seq.step("vector", lambda v, cc=cc: v.reciprocal(ysc[:, cc], yc2[:]))
        seq.step("vector", lambda v, cc=cc: v.tensor_scalar(
            vq[:, :], yt[:, :], ym[:, cc], ysc[:, cc], ALU.subtract, ALU.mult))
        seq.step("vector", lambda v: v.tensor_add(yt[:, :], vq[:], xt[0][:, :]))
        seq.step("sync", lambda s, t=t: [s.dma_start(
            out=xout[t * 128:(t + 1) * 128, :], in_=yt[:, :])], dma=True)

    @block.sync
    def _(sync):
        sync.wait_ge(seq.s.h, seq.s.n)

    return _bfin(p, standalone)


def build_l0_ffn(p=None, block=None, xin=None, xout=None, pfx=""):
    standalone = p is None
    if standalone:
        p = Prog()
    nc = p.nc
    _bsetup(p, pfx)
    if xin is None:
        xin = p.din("xin", [2048, D])
    wcT = p.din("wcT", [D, 4096])
    genes_in = p.din("genes_in", [4096, 256])
    wdT_in = p.din("wdT_in", [256, D])
    wuT_in = p.din("wuT_in", [256, D])
    nwcol_in = p.din("nwcol_in", [128, 16])
    nwrow_in = p.din("nwrow_in", [1, 2048])
    tau_in = p.din("tau_in", [1, 1])
    scale_in = p.din("scale_in", [1, 1])
    identb = p.din("identb", [128, 128], BF16)
    if xout is None:
        xout = p.dout("xout", [2048, D])
    wc2 = p.dint("wc2", [D, 4096], BF16)

    xnT = p.sbuf("xnT", [128, NT * 2048], BF16)
    lgrp = p.sbuf("lgrp", [128, 2 * 4096], BF16)
    ptb = p.sbuf("ptb", [128, 32 * 256], BF16)
    genes_bf = p.sbuf("genes_bf", [128, 32 * 256], BF16)
    wdu_bf = p.sbuf("wdu_bf", [128, 8192], BF16)
    exprT = p.sbuf("exprT", [128, 2 * 2048], BF16)
    wst = p.sbuf("wst", [128, 2048], F32)
    xt = [p.sbuf(f"xt{i}", [128, 2048], F32) for i in range(2)]
    yt = p.sbuf("yt", [128, 2048], F32)
    vq = p.sbuf("vq", [128, 2048], F32)
    xq = p.sbuf("xq", [128, 2048], BF16)
    work = p.sbuf("work", [128, 4096], BF16)
    ptile = p.sbuf("ptile", [128, 4096], BF16)
    wbf = [p.sbuf(f"wbf{i}", [128, 512], BF16) for i in range(4)]
    tanhd = p.sbuf("tanhd", [128, 1024], F32)
    ident = p.sbuf("ident", [128, 128], BF16)
    ssq = p.sbuf("ssq", [128, 16], F32)
    ssum = p.sbuf("ssum", [128, 16], F32)
    mcol = p.sbuf("mcol", [128, 16], F32)
    rstd = p.sbuf("rstd", [128, 16], F32)
    vtmp = p.sbuf("vtmp", [128, 16], F32)
    vtmp2 = p.sbuf("vtmp2", [128, 16], F32)
    scol = p.sbuf("scol", [128, 16], F32)
    nwcol = p.sbuf("nwcol", [128, 16], F32)
    wcscale = p.sbuf("wcscale", [128, 16], F32)
    tau1 = p.sbuf("tau1", [1, 1], F32)
    rt1 = p.sbuf("rt1", [1, 1], F32)
    sc1 = p.sbuf("sc1", [1, 1], F32)
    rtcol = p.sbuf("rtcol", [128, 1], F32)
    sccol = p.sbuf("sccol", [128, 1], F32)
    onecol = p.sbuf("onecol", [128, 1], F32)
    epsc = p.sbuf("epsc", [128, 1], F32)
    rmc = p.sbuf("rmc", [128, 2], F32)
    top8 = p.sbuf("top8", [128, 8], F32)
    rdc = p.sbuf("rdc", [128, 1], F32)
    ysum = p.sbuf("ysum", [128, 16], F32)
    ysq = p.sbuf("ysq", [128, 16], F32)
    ym = p.sbuf("ym", [128, 16], F32)
    ysc = p.sbuf("ysc", [128, 16], F32)
    yc1 = p.sbuf("yc1", [128, 1], F32)
    yc2 = p.sbuf("yc2", [128, 1], F32)
    yc3 = p.sbuf("yc3", [128, 1], F32)

    pt0 = p.psum("pt0", [128, 1024], BF16)
    pt1 = p.psum("pt1", [128, 1024], BF16)
    pA = p.psum("pA", [128, 2048], F32)
    pB = p.psum("pB", [128, 1024], F32)

    stX = Stream2(p, "stX", 16, n_consumers=2)
    stX2 = Stream2(p, "stX2", 16, n_consumers=1)
    stX3 = Stream2(p, "stX3", 16, n_consumers=1)
    stWB = Stream2(p, "stWB", 8 * 8 * 16, n_consumers=1, nb=4)
    s_vc = p.sem("s_vc")
    s_sc = p.sem("s_sc")
    s_tc = p.sem("s_tc")
    s_xnT = p.sem("s_xnT")
    s_wc = p.sem("s_wc")
    s_wc2 = p.sem("s_wc2")
    s_lg = p.sem("s_lg")
    s_pt = p.sem("s_pt")
    s_ptb = p.sem("s_ptb")
    s_ex = p.sem("s_ex")
    s_st = p.sem("s_st")
    s_cst = p.sem("s_cst")
    s_g = p.sem("s_g")
    s_gld = p.sem("s_gld")

    if block is None:
        block = p.es.enter_context(nc.Block())
    NG = 8

    @block.sync
    def _(sync):
        sync.dma_start(out=ident[:], in_=identb[:, :]).then_inc(s_cst.h, 16)
        sync.dma_start(out=nwcol[:], in_=nwcol_in[:, :]).then_inc(s_cst.h, 16)
        sync.dma_start(out=tau1[:], in_=tau_in[:, :]).then_inc(s_cst.h, 16)
        sync.dma_start(out=sc1[:], in_=scale_in[:, :]).then_inc(s_cst.h, 16)
        sync.dma_start(out=vq[0:1, 0:2048], in_=nwrow_in[:, :]).then_inc(s_cst.h, 16)
    s_cst.n += 80

    @block.vector
    def _(vector):
        s_vc.inc(vector.memset(epsc[:], 1e-5))
        s_vc.inc(vector.memset(onecol[:], 1.0))
        vector.wait_ge(s_cst.h, 80)
        m2 = s_vc.inc(vector.tensor_scalar(rt1[:], tau1[:], 0.1, None, ALU.max))
        vector.wait_ge(s_vc.h, m2)
        s_vc.inc(vector.reciprocal(rt1[:], rt1[:]))
    rt_mark = s_vc.n

    @block.tensor
    def _(tensor):
        tensor.wait_ge(s_vc.h, rt_mark)
        ones_row = onecol[0:1, :].to_broadcast([1, 128])
        tensor.matmul(pB[:, 0:1], ones_row, rt1[:, :], start=True, stop=True)
        t2 = tensor.matmul(pB[:, 512:513], ones_row, sc1[:, :], start=True, stop=True)
        s_tc.inc(t2)
        for n in range(4):
            t3 = tensor.matmul(pA[:, n * 512:(n + 1) * 512], ones_row,
                               vq[0:1, n * 512:(n + 1) * 512], start=True, stop=True)
        s_tc.inc(t3)
    t_bc = s_tc.n

    @block.vector
    def _(vector):
        vector.wait_ge(s_tc.h, t_bc)
        m1 = s_vc.inc(vector.tensor_copy(rtcol[:], pB[:, 0:1]))
        s_vc.inc(vector.tensor_copy(sccol[:], pB[:, 512:513]))
        s_vc.inc(vector.tensor_copy(yt[:], pA[:, :]))
        vector.wait_ge(s_vc.h, m1)
        s_vc.inc(vector.tensor_scalar(wcscale[:], nwcol[:], rtcol[:, 0:1], None,
                                      ALU.mult))
    bc_mark = s_vc.n

    @block.gpsimd
    def _(gpsimd):
        g3 = genes_in.rearrange("(c pp) e -> c pp e", pp=128)
        for r in range(4):
            if r > 0:
                gpsimd.wait_ge(s_g.h, r)
            for cc in range(8):
                gpsimd.dma_start(out=wst[:, cc * 256:(cc + 1) * 256],
                                 in_=g3[r * 8 + cc]).then_inc(s_gld.h, 16)
        s_gld.n += 32 * 16
        srcs = [wdT_in[0:128, :], wdT_in[128:256, :],
                wuT_in[0:128, :], wuT_in[128:256, :]]
        for r in range(4):
            gpsimd.wait_ge(s_g.h, 4 + r)
            gpsimd.dma_start(out=wst[:, 0:2048], in_=srcs[r]).then_inc(s_gld.h, 16)
        s_gld.n += 4 * 16

    @block.vector
    def _(vector):
        for r in range(4):
            vector.wait_ge(s_gld.h, 16 * 8 * (r + 1))
            ins = vector.tensor_copy(genes_bf[:, r * 2048:(r + 1) * 2048], wst[:, :])
            s_g.inc(ins)
        for r in range(4):
            vector.wait_ge(s_gld.h, 32 * 16 + 16 * (r + 1))
            vector.wait_ge(s_vc.h, bc_mark)
            if r < 2:
                ins = vector.tensor_copy(wdu_bf[:, r * 2048:(r + 1) * 2048],
                                         wst[:, :])
            else:
                ins = vector.tensor_mul(wdu_bf[:, r * 2048:(r + 1) * 2048],
                                        wst[:, :], yt[:, :])
            s_g.inc(ins)
    g_mark = s_g.n

    @block.sync
    def _(sync):
        for i in range(32):
            d, hh = i // 2, i % 2
            if i == 0:
                sync.wait_ge(s_g.h, 8)
            else:
                sync.wait_ge(s_wc.h, i)
            sync.dma_start(out=wst[:, :],
                           in_=wcT[d * 128:(d + 1) * 128, hh * 2048:(hh + 1) * 2048])\
                .then_inc(s_cst.h, 16)
        s_cst.n += 32 * 16
    wcld_base = s_cst.n - 32 * 16

    @block.vector
    def _(vector):
        for i in range(32):
            d, hh = i // 2, i % 2
            vector.wait_ge(s_cst.h, wcld_base + 16 * (i + 1))
            vector.wait_ge(s_vc.h, bc_mark)
            if i >= 2:
                vector.wait_ge(s_wc2.h, 16 * (i // 2))
            ins = vector.tensor_scalar(work[:, hh * 2048:(hh + 1) * 2048], wst[:, :],
                                       wcscale[:, d:d + 1], None, ALU.mult)
            s_wc.inc(ins)

    @block.gpsimd
    def _(gpsimd):
        for d in range(16):
            gpsimd.wait_ge(s_wc.h, 2 * (d + 1))
            gpsimd.dma_start(out=wc2[d * 128:(d + 1) * 128, :], in_=work[:, :])\
                .then_inc(s_wc2.h, 16)
        s_wc2.n += 256

    ln_mark = emit_ln_stats(p, block, xin, xt, stX, pA[:, :], ssq, ssum,
                            mcol, rstd, vtmp, vtmp2, s_sc, s_vc, epsc[:, 0:1])

    @block.vector
    def _(vector):
        vector.wait_ge(s_vc.h, ln_mark)
        m1 = s_vc.inc(vector.tensor_mul(vtmp[:], rstd[:], rstd[:]))
        vector.wait_ge(s_vc.h, m1)
        s_vc.inc(vector.tensor_scalar(vtmp[:], vtmp[:], -1e-5,
                                      1.0 + 1.1920929e-7, ALU.mult, ALU.add))
    sv_mark = s_vc.n

    @block.scalar
    def _(scalar):
        scalar.wait_ge(s_vc.h, sv_mark)
        s_sc.inc(scalar.sqrt(vtmp[:], vtmp[:]))
    sv2_mark = s_sc.n

    @block.vector
    def _(vector):
        vector.wait_ge(s_sc.h, sv2_mark)
        m1 = s_vc.inc(vector.reciprocal(vtmp2[:], vtmp[:]))
        vector.wait_ge(s_vc.h, m1)
        s_vc.inc(vector.tensor_mul(scol[:], rstd[:], vtmp2[:]))
    scol_mark = s_vc.n

    @block.sync
    def _(sync):
        stX2.emit_loads(sync, xt, [xin[t * 128:(t + 1) * 128, :] for t in range(16)],
                        pre_waits=[(stX.use[0].h, 16), (stX.use[1].h, 16)])

    @block.vector
    def _(vector):
        for t in range(16):
            stX2.wait_tile(vector, t)
            vector.wait_ge(s_vc.h, scol_mark)
            if t >= 1:
                vector.wait_ge(stX2.use[(t - 1) % 2].h, (t - 1) // 2 + 1)
                vector.wait_ge(s_tc.h, t_bc + 2 * t)
            ins = vector.tensor_scalar(xq[:], xt[t % 2][:], mcol[:, t:t + 1],
                                       scol[:, t:t + 1], ALU.subtract, ALU.mult)
            stX2.done_tile(ins, t)

    @block.tensor
    def _(tensor):
        tensor.wait_ge(s_cst.h, 80)
        for t in range(16):
            tensor.wait_ge(stX2.use[t % 2].h, t // 2 + 1)
            for g in range(2):
                ptg = [pt0, pt1][g]
                if t >= 1:
                    tensor.wait_ge(s_xnT.h, 2 * (t - 1) + g + 1)
                for j in range(8):
                    ch = g * 8 + j
                    ins = tensor.transpose(ptg[:, j * 128:(j + 1) * 128],
                                           xq[:, ch * 128:(ch + 1) * 128], ident[:])
                s_tc.inc(ins)
    xnt_tr_base = s_tc.n - 32

    @block.scalar
    def _(scalar):
        for t in range(16):
            for g in range(2):
                scalar.wait_ge(s_tc.h, xnt_tr_base + 2 * t + g + 1)
                dst = xnT[:, :].rearrange("pp (c n) -> pp c n", c=16)
                ins = scalar.copy(dst[:, g * 8:(g + 1) * 8, t * 128:(t + 1) * 128],
                                  [pt0, pt1][g][:, :].rearrange("pp (c n) -> pp c n",
                                                                c=8))
                s_xnT.inc(ins)

    wb_srcs = []
    for G in range(NG):
        for gc in range(8):
            for d in range(16):
                wb_srcs.append(wc2[d * 128:(d + 1) * 128, gc * 512:(gc + 1) * 512])

    @block.sync
    def _(sync):
        stWB.emit_loads(sync, wbf, wb_srcs, pre_waits=[(s_wc2.h, 256)])

    lgev = {}
    for G in range(NG):
        @block.tensor
        def _(tensor, G=G):
            for gc in range(8):
                prev_idx = G * 8 + gc - 1
                if prev_idx >= 0:
                    tensor.wait_ge(s_lg.h, 2 * (prev_idx + 1))
                else:
                    tensor.wait_ge(s_vc.h, bc_mark)
                tensor.wait_ge(s_xnT.h, 32)
                for d in range(16):
                    j = (G * 8 + gc) * 16 + d
                    stWB.wait_tile(tensor, j)
                    for tt in range(2):
                        tgl = G * 2 + tt
                        ins = tensor.matmul(
                            pA[:, tt * 512:(tt + 1) * 512],
                            xnT[:, d * 2048 + tgl * 128: d * 2048 + (tgl + 1) * 128],
                            wbf[j % 4][:, :], start=(d == 0), stop=(d == 15))
                        if tt == 1:
                            stWB.done_tile(ins, j)

        @block.scalar
        def _(scalar, G=G):
            for gc in range(8):
                j15 = (G * 8 + gc) * 16 + 15
                scalar.wait_ge(stWB.use[j15 % 4].h, j15 // 4 + 1)
                if G >= 1:
                    scalar.wait_ge(s_ptb.h, 8 * G)
                for tt in range(2):
                    ins = scalar.copy(
                        lgrp[:, tt * 4096 + gc * 512: tt * 4096 + (gc + 1) * 512],
                        pA[:, tt * 512:(tt + 1) * 512])
                    s_lg.inc(ins)
                lgev[(G, gc)] = s_lg.n

        vbase = s_vc.n

        @block.scalar
        def _(scalar, G=G, vbase=vbase):
            for tt in range(2):
                scalar.wait_ge(s_vc.h, vbase + 14 * tt + 12)
                ins = scalar.activation(out=work[:, :],
                                        in_=lgrp[:, tt * 4096:(tt + 1) * 4096],
                                        func=AF.Exp, bias=rmc[:, tt:tt + 1])
                s_sc.inc(ins)
        expm = {tt: s_sc.n - 2 + tt + 1 for tt in range(2)}

        @block.vector
        def _(vector, G=G, vbase=vbase, expm=expm):
            for tt in range(2):
                sl = lgrp[:, tt * 4096:(tt + 1) * 4096]
                vector.wait_ge(s_lg.h, lgev[(G, 7)])
                c1 = s_vc.inc(vector.reduce_max(rmc[:, tt:tt + 1], sl, axis=AX.X))
                vector.wait_ge(s_vc.h, c1)
                c2 = s_vc.inc(vector.tensor_scalar(rmc[:, tt:tt + 1],
                                                   rmc[:, tt:tt + 1],
                                                   -1.0, None, ALU.mult))
                c3 = s_vc.inc(vector.tensor_copy(work[:, :], sl))
                assert c3 == vbase + 14 * tt + 3
                vector.wait_ge(s_vc.h, c3)
                for r in range(4):
                    r1 = s_vc.inc(vector.max(out=top8[:], in_=work[:, :]))
                    vector.wait_ge(s_vc.h, r1)
                    r2 = s_vc.inc(vector.match_replace(out=work[:, :],
                                                       in_to_replace=top8[:],
                                                       in_values=work[:, :],
                                                       imm_value=NEG))
                    vector.wait_ge(s_vc.h, r2)
                c4 = s_vc.inc(vector.tensor_tensor(ptile[:, :], sl, work[:, :],
                                                   ALU.not_equal))
                assert c4 == vbase + 14 * tt + 12
                vector.wait_ge(s_vc.h, c4)
                vector.wait_ge(s_sc.h, expm[tt])
                c5 = s_vc.inc(vector.scalar_tensor_tensor(
                    ptile[:, :], ptile[:, :], 1.0, work[:, :],
                    ALU.mult, ALU.mult, accum_out=rdc[:]))
                vector.wait_ge(s_vc.h, c5)
                c6 = s_vc.inc(vector.reciprocal(rdc[:], rdc[:]))
                vector.wait_ge(s_vc.h, c6)
                ins = vector.tensor_scalar(sl, ptile[:, :], rdc[:, 0:1], None,
                                           ALU.mult)
                s_pt.inc(ins)

        @block.tensor
        def _(tensor, G=G):
            tensor.wait_ge(s_pt.h, 2 * (G + 1))
            for bi in range(8):
                ptg = [pt0, pt1][bi % 2]
                prev = G * 8 + bi - 2
                if G == 0 and bi < 2:
                    tensor.wait_ge(s_xnT.h, 32)
                elif prev >= 0:
                    tensor.wait_ge(s_ptb.h, prev + 1)
                for k in range(8):
                    b = bi * 8 + k
                    gc, tt = b // 2, b % 2
                    ins = tensor.transpose(
                        ptg[:, k * 128:(k + 1) * 128],
                        lgrp[:, tt * 4096 + gc * 128: tt * 4096 + (gc + 1) * 128],
                        ident[:])
                s_tc.inc(ins)
        tr_base = s_tc.n - 8

        @block.vector
        def _(vector, G=G, tr_base=tr_base):
            for bi in range(8):
                vector.wait_ge(s_tc.h, tr_base + bi + 1)
                ins = vector.tensor_copy(ptb[:, bi * 1024:(bi + 1) * 1024],
                                         [pt0, pt1][bi % 2][:, :])
                s_ptb.inc(ins)

        @block.tensor
        def _(tensor, G=G):
            tensor.wait_ge(s_ptb.h, 8 * (G + 1))
            tensor.wait_ge(s_g.h, 8)
            if G >= 1:
                tensor.wait_ge(s_ex.h, 2 * G)
            else:
                tensor.wait_ge(s_vc.h, bc_mark)
            for cb in range(2):
                for gc in range(32):
                    ins = tensor.matmul(
                        pB[:, cb * 512:cb * 512 + 256],
                        genes_bf[:, gc * 256 + cb * 128: gc * 256 + (cb + 1) * 128],
                        ptb[:, gc * 256:(gc + 1) * 256],
                        start=(gc == 0), stop=(gc == 31))
                s_tc.inc(ins)
        exmm = s_tc.n

        @block.scalar
        def _(scalar, G=G, exmm=exmm):
            for cb in range(2):
                scalar.wait_ge(s_tc.h, exmm - 2 + cb + 1)
                ins = scalar.copy(
                    exprT[:, cb * 2048 + G * 256: cb * 2048 + (G + 1) * 256],
                    pB[:, cb * 512:cb * 512 + 256])
                s_ex.inc(ins)

    @block.sync
    def _(sync):
        stX3.emit_loads(sync, xt, [xin[t * 128:(t + 1) * 128, :] for t in range(16)],
                        pre_waits=[(stX2.use[0].h, 8), (stX2.use[1].h, 8)])

    sc_ep = s_sc.n
    tan_pred = {(t, h): sc_ep + 4 * t + h + 1 for t in range(16) for h in range(2)}
    stat_pred = {t: sc_ep + 4 * t + 3 for t in range(16)}
    sqrt_pred = {t: sc_ep + 4 * t + 4 for t in range(16)}
    vc_ep = s_vc.n
    duv_pred = {(t, h): vc_ep + 15 * t + 3 * (h + 1) for t in range(16)
                for h in range(2)}
    var_pred = {t: vc_ep + 15 * t + 12 for t in range(16)}
    dut = {}

    @block.tensor
    def _(tensor):
        tensor.wait_ge(s_ex.h, 2 * NG)
        tensor.wait_ge(s_lg.h, 2 * 8 * NG)
        tensor.wait_ge(s_ptb.h, 8 * NG)
        for t in range(16):
            for h in range(2):
                if (t, h) != (0, 0):
                    ph = (t, 0) if h == 1 else (t - 1, 1)
                    tensor.wait_ge(s_vc.h, duv_pred[ph])
                if t >= 1:
                    tensor.wait_ge(s_sc.h, tan_pred[(t - 1, h)])
                    tensor.wait_ge(s_sc.h, stat_pred[t - 1])
                for n in range(2):
                    for cb in range(2):
                        tensor.matmul(
                            pA[:, h * 1024 + n * 512: h * 1024 + (n + 1) * 512],
                            exprT[:, cb * 2048 + t * 128:cb * 2048 + (t + 1) * 128],
                            wdu_bf[:, cb * 2048 + h * 1024 + n * 512:
                                   cb * 2048 + h * 1024 + (n + 1) * 512],
                            start=(cb == 0), stop=(cb == 1))
                for n in range(2):
                    for cb in range(2):
                        ins = tensor.matmul(
                            pB[:, n * 512:(n + 1) * 512],
                            exprT[:, cb * 2048 + t * 128:cb * 2048 + (t + 1) * 128],
                            wdu_bf[:, 4096 + cb * 2048 + h * 1024 + n * 512:
                                   4096 + cb * 2048 + h * 1024 + (n + 1) * 512],
                            start=(cb == 0), stop=(cb == 1))
                dut[(t, h)] = s_tc.inc(ins)

    @block.scalar
    def _(scalar):
        for t in range(16):
            for h in range(2):
                scalar.wait_ge(s_tc.h, dut[(t, h)])
                if h == 1:
                    scalar.wait_ge(s_vc.h, duv_pred[(t, 0)])
                hs = slice(h * 1024, (h + 1) * 1024)
                ins = scalar.activation(out=tanhd[:, 0:1024], in_=pA[:, hs],
                                        func=AF.Tanh)
                m = s_sc.inc(ins)
                assert m == tan_pred[(t, h)]
            scalar.wait_ge(s_vc.h, duv_pred[(t, 1)])
            ins = scalar.activation(out=pA[:, :], in_=yt[:, :], func=AF.Square,
                                    accum_out=ysq[:, t:t + 1])
            m = s_sc.inc(ins)
            assert m == stat_pred[t]
            scalar.wait_ge(s_vc.h, var_pred[t])
            ins2 = scalar.activation(out=yc2[:], in_=yc1[:], func=AF.Sqrt,
                                     bias=epsc[:, 0:1])
            m = s_sc.inc(ins2)
            assert m == sqrt_pred[t]

    @block.vector
    def _(vector):
        for t in range(16):
            for h in range(2):
                hs = slice(h * 1024, (h + 1) * 1024)
                stX3.wait_tile(vector, t)
                vector.wait_ge(s_sc.h, tan_pred[(t, h)])
                vector.wait_ge(s_vc.h, scol_mark)
                if t >= 1 and h == 0:
                    vector.wait_ge(s_st.h, 16 * t)
                m1 = s_vc.inc(vector.tensor_scalar(vq[:, hs], xt[t % 2][:, hs],
                                                   mcol[:, t:t + 1], scol[:, t:t + 1],
                                                   ALU.subtract, ALU.mult))
                vector.wait_ge(s_vc.h, m1)
                m3 = s_vc.inc(vector.tensor_mul(yt[:, hs], tanhd[:, 0:1024],
                                                pB[:, 0:1024]))
                vector.wait_ge(s_vc.h, m3)
                ins = vector.tensor_mul(yt[:, hs], yt[:, hs], vq[:, hs])
                m4 = s_vc.inc(ins)
                assert m4 == duv_pred[(t, h)]
            c = slice(t, t + 1)
            vector.wait_ge(s_vc.h, duv_pred[(t, 1)])
            v1 = s_vc.inc(vector.reduce_sum(ysum[:, c], yt[:, :], axis=AX.X))
            vector.wait_ge(s_vc.h, v1)
            vector.wait_ge(s_sc.h, stat_pred[t])
            v2 = s_vc.inc(vector.tensor_scalar(ym[:, c], ysum[:, c], 1.0 / D,
                                               None, ALU.mult))
            vector.wait_ge(s_vc.h, v2)
            v3 = s_vc.inc(vector.tensor_mul(yc1[:], ym[:, c], ym[:, c]))
            vector.wait_ge(s_vc.h, v3)
            v4 = s_vc.inc(vector.scalar_tensor_tensor(yc1[:], ysq[:, c], 1.0 / D,
                                                      yc1[:], ALU.mult, ALU.subtract))
            v4b = s_vc.inc(vector.tensor_mul(yc3[:], sccol[:, 0:1], sccol[:, 0:1]))
            vector.wait_ge(s_vc.h, max(v4, v4b))
            v5 = s_vc.inc(vector.tensor_mul(yc1[:], yc1[:], yc3[:]))
            assert v5 == var_pred[t]
            vector.wait_ge(s_sc.h, sqrt_pred[t])
            v6 = s_vc.inc(vector.reciprocal(yc2[:], yc2[:]))
            vector.wait_ge(s_vc.h, v6)
            v7 = s_vc.inc(vector.tensor_mul(ysc[:, c], yc2[:], sccol[:, 0:1]))
            vector.wait_ge(s_vc.h, v7)
            v8 = s_vc.inc(vector.tensor_scalar(vq[:], yt[:, :], ym[:, c],
                                               ysc[:, c], ALU.subtract, ALU.mult))
            vector.wait_ge(s_vc.h, v8)
            ins = vector.tensor_add(yt[:, :], vq[:], xt[t % 2][:])
            stX3.done_tile(ins, t)

    @block.gpsimd
    def _(gpsimd):
        for t in range(16):
            gpsimd.wait_ge(stX3.use[t % 2].h, t // 2 + 1)
            gpsimd.dma_start(out=xout[t * 128:(t + 1) * 128, :], in_=yt[:, :])\
                .then_inc(s_st.h, 16)
        s_st.n += 256
        gpsimd.wait_ge(s_st.h, s_st.n)

    return _bfin(p, standalone)


SC128 = 0.08838834764831845
PI = 3.141592653589793
TWO_PI = 6.283185307179586


def build_l1_attn(p=None, block=None, xin=None, xout=None, pfx=""):
    standalone = p is None
    if standalone:
        p = Prog()
    nc = p.nc
    _bsetup(p, pfx)
    if xin is None:
        xin = p.din("xin", [2048, D])
    wqT = p.din("wqT", [D, D])
    wkT = p.din("wkT", [D, D])
    wvT = p.din("wvT", [D, D])
    woT = p.din("woT", [D, D])
    gate_in = p.din("gate_in", [1, 16])
    tvals_in = p.din("tvals_in", [1, 2048])
    invf_in = p.din("invf_in", [1, 1024])
    identb = p.din("identb", [128, 128], BF16)
    identf_in = p.din("identf_in", [128, 128], F32)
    if xout is None:
        xout = p.dout("xout", [2048, D])
    w2 = {n: p.dint(f"w2{n}", [D, D], BF16)
          for n in "qkvo"}
    qt2 = p.dint("qt2", [D, 2048], BF16)
    qf2 = p.dint("qf2", [D, 2048], BF16)
    kt2 = p.dint("kt2", [D, 2048], BF16)
    kf2 = p.dint("kf2", [2048, D], BF16)
    v2 = p.dint("v2", [2048, D], BF16)
    cb2 = p.dint("cb2", [D, 2048], BF16)
    ropes2 = p.dint("ropes2", [2048, 1024], F32)
    ropec2 = p.dint("ropec2", [2048, 1024], F32)

    xnT = p.sbuf("xnT", [128, NT * 2048], BF16)
    wh = p.sbuf("wh", [128, 16 * 1024], BF16)
    h1b = p.sbuf("h1b", [128, 16 * 1024], BF16)
    stg = p.sbuf("stg", [128, 2048], BF16)
    tile_b = p.sbuf("tile_b", [128, 2048], BF16)
    xt = [p.sbuf(f"xt{i}", [128, 2048], F32) for i in range(2)]
    xq = p.sbuf("xq", [128, 2048], BF16)
    vq = p.sbuf("vq", [128, 2048], F32)
    yt = p.sbuf("yt", [128, 2048], F32)
    rope_c = p.sbuf("rope_c", [128, 1024], F32)
    rope_s = p.sbuf("rope_s", [128, 1024], F32)
    qch = p.sbuf("qch", [128, 512], BF16)
    qfc = p.sbuf("qfc", [128, 512], BF16)
    ptx4 = p.sbuf("ptx4", [128, 2048], BF16)
    onesb = p.sbuf("onesb", [128, 1], BF16)
    memsb = p.sbuf("memsb", [128, 128], BF16)
    nrmb = p.sbuf("nrmb", [128, 1], BF16)
    cmb = p.sbuf("cmb", [128, 512], BF16)
    cblk = p.sbuf("cblk", [128, 128], BF16)
    dnr = p.sbuf("dnr", [1, 512], F32)
    dnr2 = p.sbuf("dnr2", [1, 512], F32)
    grow = p.sbuf("grow", [1, 16], F32)
    onesrw = p.sbuf("onesrw", [1, 128], F32)
    gc128 = p.sbuf("gc128", [1, 128], F32)
    g1c128 = p.sbuf("g1c128", [1, 128], F32)
    g1row = p.sbuf("g1row", [1, 16], F32)
    ident = p.sbuf("ident", [128, 128], BF16)
    identf = p.sbuf("identf", [128, 128], F32)
    tvals = p.sbuf("tvals", [1, 2048], F32)
    invf = p.sbuf("invf", [1, 1024], F32)
    ssq = p.sbuf("ssq", [128, 16], F32)
    ssum = p.sbuf("ssum", [128, 16], F32)
    mcol = p.sbuf("mcol", [128, 16], F32)
    rstd = p.sbuf("rstd", [128, 16], F32)
    vtmp = p.sbuf("vtmp", [128, 16], F32)
    vtmp2 = p.sbuf("vtmp2", [128, 16], F32)
    qn2 = p.sbuf("qn2", [128, 16], F32)
    kn2 = p.sbuf("kn2", [128, 16], F32)
    nsc = p.sbuf("nsc", [128, 16], F32)
    redc = p.sbuf("redc", [128, 2], F32)
    sm1 = p.sbuf("sm1", [1, 128], F32)
    bnd = p.sbuf("bnd", [128, 1], F32)
    b11 = p.sbuf("b11", [1, 2], F32)
    epsc = p.sbuf("epsc", [128, 1], F32)
    pibias = p.sbuf("pibias", [128, 1], F32)
    pi2col = p.sbuf("pi2col", [128, 1], F32)
    onecol = p.sbuf("onecol", [128, 1], F32)
    ysum = p.sbuf("ysum", [128, 16], F32)
    ysq = p.sbuf("ysq", [128, 16], F32)
    ym = p.sbuf("ym", [128, 16], F32)
    ysc = p.sbuf("ysc", [128, 16], F32)
    yc1 = p.sbuf("yc1", [128, 1], F32)
    yc2 = p.sbuf("yc2", [128, 1], F32)

    pt0 = p.psum("pt0", [128, 1024], BF16)
    pt1 = p.psum("pt1", [128, 1024], BF16)
    pA = p.psum("pA", [128, 2048], F32)
    pB = p.psum("pB", [128, 1024], F32)

    stX = Stream2(p, "stX", 16, n_consumers=2)
    stX2 = Stream2(p, "stX2", 16, n_consumers=1)
    s_vc = p.sem("s_vc")
    s_sc = p.sem("s_sc")
    s_tc = p.sem("s_tc")
    s_cst = p.sem("s_cst")
    s_w2 = p.sem("s_w2")
    s_wld = p.sem("s_wld")
    s_slc = p.sem("s_slc")
    s_out = p.sem("s_out")
    s_st = p.sem("s_st")

    if block is None:
        block = p.es.enter_context(nc.Block())

    @block.sync
    def _(sync):
        sync.dma_start(out=ident[:], in_=identb[:, :]).then_inc(s_cst.h, 16)
        sync.dma_start(out=grow[:], in_=gate_in[:, :]).then_inc(s_cst.h, 16)
        sync.dma_start(out=tvals[:], in_=tvals_in[:, :]).then_inc(s_cst.h, 16)
        sync.dma_start(out=invf[:], in_=invf_in[:, :]).then_inc(s_cst.h, 16)
        sync.dma_start(out=identf[:], in_=identf_in[:, :]).then_inc(s_cst.h, 16)
    s_cst.n += 80

    @block.vector
    def _(vector):
        s_vc.inc(vector.memset(epsc[:], 1e-5))
        s_vc.inc(vector.memset(onecol[:], 1.0))
        s_vc.inc(vector.memset(onesb[:], 1.0))
        s_vc.inc(vector.memset(onesrw[:], 1.0))
        s_vc.inc(vector.memset(qn2[:], 0.0))
        s_vc.inc(vector.memset(kn2[:], 0.0))
        s_vc.inc(vector.memset(pibias[:], -PI))
        s_vc.inc(vector.memset(pi2col[:], PI / 2))

    @block.scalar
    def _(scalar):
        scalar.wait_ge(s_cst.h, 80)
        s_sc.inc(scalar.activation(out=grow[:], in_=grow[:], func=AF.Sigmoid))
    gsig_mark = s_sc.n

    @block.vector
    def _(vector):
        vector.wait_ge(s_sc.h, gsig_mark)
        s_vc.inc(vector.tensor_scalar(g1row[:], grow[:], -1.0, 1.0,
                                      ALU.mult, ALU.add))
    gate_mark = s_vc.n

    @block.sync
    def _(sync):
        for i in range(64):
            if i >= 1:
                sync.wait_ge(s_w2.h, i)
            src = [wqT, wkT, wvT, woT][i % 4]
            d = i // 4
            sync.dma_start(out=vq[:, :], in_=src[d * 128:(d + 1) * 128, :])\
                .then_inc(s_wld.h, 16)
        s_wld.n += 64 * 16

    @block.vector
    def _(vector):
        for i in range(64):
            vector.wait_ge(s_wld.h, 16 * (i + 1))
            if i >= 1:
                vector.wait_ge(s_out.h, 16 * i)
            ins = vector.tensor_copy(xq[:, :], vq[:, :])
            s_w2.inc(ins)

    @block.scalar
    def _(scalar):
        for i in range(64):
            scalar.wait_ge(s_w2.h, i + 1)
            nm, d = "qkvo"[i % 4], i // 4
            scalar.dma_start(out=w2[nm][d * 128:(d + 1) * 128, :], in_=xq[:, :])\
                .then_inc(s_out.h, 16)
        s_out.n += 64 * 16
    wcache_done = s_out.n

    ln_mark = emit_ln_stats(p, block, xin, xt, stX, pA[:, :], ssq, ssum,
                            mcol, rstd, vtmp, vtmp2, s_sc, s_vc, epsc[:, 0:1])

    @block.sync
    def _(sync):
        stX2.emit_loads(sync, xt, [xin[t * 128:(t + 1) * 128, :] for t in range(16)],
                        pre_waits=[(stX.use[0].h, 16), (stX.use[1].h, 16)])

    xnt_tr0 = s_tc.n

    @block.vector
    def _(vector):
        for t in range(16):
            stX2.wait_tile(vector, t)
            vector.wait_ge(s_vc.h, ln_mark)
            vector.wait_ge(s_out.h, wcache_done)
            if t >= 1:
                vector.wait_ge(stX2.use[(t - 1) % 2].h, (t - 1) // 2 + 1)
                vector.wait_ge(s_tc.h, xnt_tr0 + 2 * t)
            ins = vector.tensor_scalar(xq[:], xt[t % 2][:], mcol[:, t:t + 1],
                                       rstd[:, t:t + 1], ALU.subtract, ALU.mult)
            stX2.done_tile(ins, t)

    @block.tensor
    def _(tensor):
        tensor.wait_ge(s_cst.h, 80)
        for t in range(16):
            tensor.wait_ge(stX2.use[t % 2].h, t // 2 + 1)
            for g in range(2):
                ptg = [pt0, pt1][g]
                if t >= 1:
                    tensor.wait_ge(s_slc.h, 2 * (t - 1) + g + 1)
                for j in range(8):
                    ch = g * 8 + j
                    ins = tensor.transpose(ptg[:, j * 128:(j + 1) * 128],
                                           xq[:, ch * 128:(ch + 1) * 128], ident[:])
                s_tc.inc(ins)

    @block.scalar
    def _(scalar):
        for t in range(16):
            for g in range(2):
                scalar.wait_ge(s_tc.h, xnt_tr0 + 2 * t + g + 1)
                dst = xnT[:, :].rearrange("pp (c n) -> pp c n", c=16)
                ins = scalar.copy(dst[:, g * 8:(g + 1) * 8, t * 128:(t + 1) * 128],
                                  [pt0, pt1][g][:, :].rearrange("pp (c n) -> pp c n",
                                                                c=8))
                s_slc.inc(ins)
    xnT_done = s_slc.n

    seq = Seq(p, block, "lock")
    seq.step("vector", lambda v: v.engine_nop(),
             extra_waits=[(s_slc.h, xnT_done), (s_vc.h, gate_mark)])

    def mmgroup(tensor, dst, lhsT_of, rhs_of, nk, nn, startk=True):
        ins = None
        for d in range(nk):
            for n in range(nn):
                ins = tensor.matmul(dst(n), lhsT_of(d), rhs_of(d, n),
                                    start=(d == 0 and startk), stop=(d == nk - 1))
        return ins

    for name in "qkv":
        rope = name in "qk"
        for H in range(2):
            seq.step("sync", lambda s, name=name, H=H: [
                s.dma_start(out=wh[:, c * 1024:(c + 1) * 1024],
                            in_=w2[name][c * 128:(c + 1) * 128,
                                         H * 1024:(H + 1) * 1024])
                for c in range(16)], dma=True)
            for t in range(16):
                seq.step("tensor", lambda tensor, t=t: mmgroup(
                    tensor,
                    lambda n: pA[:, n * 512:(n + 1) * 512],
                    lambda d: xnT[:, d * 2048 + t * 128: d * 2048 + (t + 1) * 128],
                    lambda d, n: wh[:, d * 1024 + n * 512: d * 1024 + (n + 1) * 512],
                    16, 2))
                if H == 0:
                    seq.step("scalar", lambda scalar, t=t: scalar.copy(
                        h1b[:, t * 1024:(t + 1) * 1024], pA[:, 0:1024]))
                    continue
                h1t = h1b[:, t * 1024:(t + 1) * 1024]
                if rope:
                    if name == "q":
                        seq.step("tensor", lambda tensor, t=t: [
                            tensor.matmul(pB[:, 0:512],
                                          tvals[0:1, t * 128:(t + 1) * 128],
                                          invf[0:1, 0:512], start=True, stop=True),
                            tensor.matmul(pB[:, 512:1024],
                                          tvals[0:1, t * 128:(t + 1) * 128],
                                          invf[0:1, 512:1024], start=True,
                                          stop=True)][-1])
                        seq.mstep("vector", lambda v: [
                            v.tensor_scalar(vq[:, 0:1024], pB[:, :], 1.0 / TWO_PI,
                                            MAGIC, ALU.mult, ALU.add),
                            v.tensor_scalar(yt[:, 0:1024], pB[:, :], 1.0 / TWO_PI,
                                            0.25, ALU.mult, ALU.add)])
                        seq.mstep("vector", lambda v: [
                            v.tensor_scalar(vq[:, 0:1024], vq[:, 0:1024], MAGIC,
                                            None, ALU.subtract),
                            v.tensor_scalar(yt[:, 0:1024], yt[:, 0:1024], MAGIC,
                                            MAGIC, ALU.add, ALU.subtract)])
                        seq.mstep("vector", lambda v: [
                            v.scalar_tensor_tensor(vq[:, 0:1024], vq[:, 0:1024],
                                                   -TWO_PI, pB[:, :], ALU.mult,
                                                   ALU.add),
                            v.scalar_tensor_tensor(yt[:, 0:1024], yt[:, 0:1024],
                                                   -TWO_PI, pB[:, :], ALU.mult,
                                                   ALU.add)])
                        seq.mstep("scalar", lambda s: [
                            s.activation(out=rope_s[:], in_=vq[:, 0:1024],
                                         func=AF.Sin),
                            s.activation(out=rope_c[:], in_=yt[:, 0:1024],
                                         func=AF.Sin, bias=pi2col[:, 0:1])])
                    else:
                        seq.step("sync", lambda s, t=t: [
                            s.dma_start(out=rope_s[:],
                                        in_=ropes2[t * 128:(t + 1) * 128, :]),
                            s.dma_start(out=rope_c[:],
                                        in_=ropec2[t * 128:(t + 1) * 128, :])],
                            dma=True)
                    seq.mstep("vector", lambda v, h1t=h1t: [
                        v.tensor_mul(vq[:, 0:1024], h1t, rope_c[:]),
                        v.tensor_mul(vq[:, 1024:2048], pA[:, 0:1024], rope_s[:]),
                        v.tensor_mul(yt[:, 0:1024], pA[:, 0:1024], rope_c[:]),
                        v.tensor_mul(yt[:, 1024:2048], h1t, rope_s[:])])
                    seq.mstep("vector", lambda v: [
                        v.tensor_sub(tile_b[:, 0:1024], vq[:, 0:1024],
                                     vq[:, 1024:2048]),
                        v.tensor_add(tile_b[:, 1024:2048], yt[:, 0:1024],
                                     yt[:, 1024:2048])])
                    seq.step("vector", lambda v: v.tensor_mul(
                        vq[:, :], tile_b[:, :], tile_b[:, :]))
                    seq.step("vector", lambda v: v.reduce_sum(
                        nsc[:], vq[:, :].rearrange("pp (hh e) -> pp hh e", hh=16),
                        axis=AX.X))
                    dstn = qn2 if name == "q" else kn2
                    seq.step("vector", lambda v, dstn=dstn: v.tensor_max(
                        dstn[:], dstn[:], nsc[:]))
                    if name == "k":
                        seq.step("vector", lambda v: v.tensor_scalar(
                            vq[:, :], tile_b[:, :], 0.0, None, ALU.min))
                        seq.step("scalar", lambda s: s.activation(
                            out=yt[:, :], in_=vq[:, :], func=AF.Exp))
                        seq.step("vector", lambda v: v.scalar_tensor_tensor(
                            xq[:, :], tile_b[:, :], 0.0, yt[:, :],
                            ALU.max, ALU.add))
                        seq.step("sync", lambda s, t=t: [s.dma_start(
                            out=kf2[t * 128:(t + 1) * 128, :], in_=xq[:, :])],
                            dma=True)
                else:
                    seq.step("scalar", lambda s, h1t=h1t: s.copy(
                        tile_b[:, 0:1024], h1t))
                    seq.step("scalar", lambda s: s.copy(
                        tile_b[:, 1024:2048], pA[:, 0:1024]))
                    seq.step("sync", lambda s, t=t: [s.dma_start(
                        out=v2[t * 128:(t + 1) * 128, :], in_=tile_b[:, :])],
                        dma=True)
                    continue
                seq.step("tensor", lambda tensor: [
                    tensor.transpose([pt0, pt1][g][:, j * 128:(j + 1) * 128],
                                     tile_b[:, (g * 8 + j) * 128:
                                            (g * 8 + j + 1) * 128], ident[:])
                    for g in range(2) for j in range(8)][-1])
                sc = SC128 if name == "k" else 1.0
                seq.mstep("vector", lambda v, sc=sc: [
                    v.tensor_scalar(stg[:, 0:1024], pt0[:, :], sc, None,
                                    ALU.mult),
                    v.tensor_scalar(stg[:, 1024:2048], pt1[:, :], sc, None,
                                    ALU.mult)])
                dst2 = kt2 if name == "k" else qt2
                seq.step("sync", lambda s, t=t, dst2=dst2, name=name: [s.dma_start(
                    out=dst2.rearrange("(c pp) tt -> pp c tt", pp=128)[
                        :, :, t * 128:(t + 1) * 128],
                    in_=stg[:, :].rearrange("pp (c j) -> pp c j", c=16))] + ([
                    s.dma_start(out=ropes2[t * 128:(t + 1) * 128, :],
                                in_=rope_s[:]),
                    s.dma_start(out=ropec2[t * 128:(t + 1) * 128, :],
                                in_=rope_c[:])] if name == "q" else []),
                    dma=True)
                if name == "q":
                    seq.step("vector", lambda v: v.tensor_scalar(
                        vq[:, :], stg[:, :], 0.0, None, ALU.min))
                    seq.step("scalar", lambda s: s.activation(
                        out=yt[:, :], in_=vq[:, :], func=AF.Exp))
                    seq.step("vector", lambda v: v.scalar_tensor_tensor(
                        xq[:, :], stg[:, :], 0.0, yt[:, :], ALU.max, ALU.add))
                    seq.step("sync", lambda s, t=t: [s.dma_start(
                        out=qf2.rearrange("(c pp) tt -> pp c tt", pp=128)[
                            :, :, t * 128:(t + 1) * 128],
                        in_=xq[:, :].rearrange("pp (c j) -> pp c j", c=16))],
                        dma=True)

    seq.step("vector", lambda v: v.reduce_max(redc[:, 0:1], qn2[:], axis=AX.X))
    seq.step("vector", lambda v: v.reduce_max(redc[:, 1:2], kn2[:], axis=AX.X))
    seq.step("vector", lambda v: v.tensor_copy(nsc[0:64, 0:2], redc[64:128, :]))
    seq.step("vector", lambda v: v.tensor_max(redc[0:64, :], redc[0:64, :],
                                              nsc[0:64, 0:2]))
    seq.step("vector", lambda v: v.tensor_copy(nsc[0:32, 2:4], redc[32:64, :]))
    seq.step("vector", lambda v: v.tensor_max(redc[0:32, :], redc[0:32, :],
                                              nsc[0:32, 2:4]))
    seq.step("tensor", lambda tensor: tensor.transpose(
        pA[0:1, 0:32], redc[0:32, 0:1], identf[0:32, 0:32]))
    seq.step("tensor", lambda tensor: tensor.transpose(
        pA[0:1, 32:64], redc[0:32, 1:2], identf[0:32, 0:32]))
    seq.step("vector", lambda v: v.tensor_copy(sm1[0:1, 0:64], pA[0:1, 0:64]))
    seq.step("vector", lambda v: v.reduce_max(b11[0:1, 0:1],
                                              sm1[0:1, 0:32], axis=AX.X))
    seq.step("vector", lambda v: v.reduce_max(b11[0:1, 1:2],
                                              sm1[0:1, 32:64], axis=AX.X))
    seq.step("vector", lambda v: v.tensor_mul(b11[0:1, 0:1], b11[0:1, 0:1],
                                              b11[0:1, 1:2]))
    seq.step("scalar", lambda s: s.sqrt(b11[0:1, 0:1], b11[0:1, 0:1]))
    seq.step("vector", lambda v: v.tensor_scalar(b11[0:1, 0:1], b11[0:1, 0:1],
                                                 -SC128, None, ALU.mult))
    seq.step("tensor", lambda tensor: tensor.matmul(
        pB[:, 0:1], onecol[0:1, :].to_broadcast([1, 128]), b11[0:1, 0:1],
        start=True, stop=True))
    seq.step("vector", lambda v: v.tensor_copy(bnd[:], pB[:, 0:1]))

    _pendst = [None]

    def _takest(s):
        if _pendst[0] is None:
            return []
        hs_, qc_ = _pendst[0]
        _pendst[0] = None
        return [s.dma_start(out=cb2[hs_, qc_ * 512:(qc_ + 1) * 512],
                            in_=cmb[:, :])]
    for h in range(16):
        hs = slice(h * 128, (h + 1) * 128)
        v3 = v2.rearrange("(tc pp) j -> pp tc j", pp=128)
        f3 = kf2.rearrange("(tc pp) j -> pp tc j", pp=128)
        seq.step("sync", lambda s, hs=hs: _takest(s) + [
            s.dma_start(out=tile_b[:, :], in_=kt2[hs, :]),
            s.dma_start(out=stg[:, :].rearrange("pp (tc j) -> pp tc j", tc=16),
                        in_=v3[:, :, hs]),
            s.dma_start(out=xq[:, :].rearrange("pp (tc j) -> pp tc j", tc=16),
                        in_=f3[:, :, hs])], dma=True)
        seq.step("tensor", lambda tensor: [x for t in range(16) for x in (
            tensor.matmul(pA[:, 0:128], xq[:, t * 128:(t + 1) * 128],
                          stg[:, t * 128:(t + 1) * 128],
                          start=(t == 0), stop=(t == 15)),
            tensor.matmul(pB[0:1, 0:128], onesb[:, :],
                          xq[:, t * 128:(t + 1) * 128],
                          start=(t == 0), stop=(t == 15),
                          skip_group_check=True))][-1])
        seq.step("scalar", lambda s: s.copy(memsb[:, :], pA[:, 0:128]))
        seq.step("vector", lambda v: v.tensor_copy(sm1[0:1, 0:128],
                                                   pB[0:1, 0:128]))
        seq.step("tensor", lambda tensor: tensor.transpose(
            pA[:, 1536:1537], sm1[0:1, 0:128], identf[0:1, 0:1]))
        seq.mstep("vector", lambda v, h=h: [
            v.tensor_copy(nrmb[:, 0:1], pA[:, 1536:1537]),
            v.tensor_scalar(gc128[0:1, :], onesrw[0:1, :],
                            grow[0:1, h:h + 1], None, ALU.mult),
            v.tensor_scalar(g1c128[0:1, :], onesrw[0:1, :],
                            g1row[0:1, h:h + 1], None, ALU.mult)])
        for qc in range(4):
            qs = slice(qc * 512, (qc + 1) * 512)
            seq.step("sync", lambda s, qc=qc, hs=hs: _takest(s) + [
                s.dma_start(out=qfc[:, :], in_=qf2[hs, qc * 512:(qc + 1) * 512]),
                s.dma_start(out=qch[:, :], in_=qt2[hs, qc * 512:(qc + 1) * 512])],
                dma=True)
            seq.step("tensor", lambda tensor: [
                tensor.matmul(pA[:, 1024:1536], memsb[:, :], qfc[:, :],
                              start=True, stop=True),
                tensor.matmul(pB[0:1, 512:1024], nrmb[:, 0:1], qfc[:, :],
                              start=True, stop=True)][-1])
            seq.mstep("vector", lambda v: [
                v.tensor_copy(vq[:, 1536:2048], pA[:, 1024:1536]),
                v.reciprocal(dnr[:], pB[0:1, 512:1024])])
            def _scores(tensor, g4):
                return [tensor.matmul(pA[:, i * 512:(i + 1) * 512],
                                      tile_b[:, (g4 * 4 + i) * 128:
                                             (g4 * 4 + i + 1) * 128],
                                      qch[:, :], start=True, stop=True)
                        for i in range(4)]

            def _pv(tensor, g4):
                return [x for i in range(4) for x in (
                    tensor.matmul(pB[0:1, 512:1024], onesb[:, :],
                                  ptx4[:, i * 512:(i + 1) * 512],
                                  start=(g4 == 0 and i == 0),
                                  stop=(g4 == 3 and i == 3),
                                  skip_group_check=True),
                    tensor.matmul(pB[:, 0:512],
                                  stg[:, (g4 * 4 + i) * 128:
                                      (g4 * 4 + i + 1) * 128],
                                  ptx4[:, i * 512:(i + 1) * 512],
                                  start=(g4 == 0 and i == 0),
                                  stop=(g4 == 3 and i == 3),
                                  skip_group_check=True))]

            seq.step("tensor", lambda tensor: _scores(tensor, 0)[-1])
            for g4 in range(4):
                seq.step("scalar", lambda s: s.activation(
                    out=ptx4[:, :], in_=pA[:, 0:2048], func=AF.Exp,
                    bias=bnd[:, 0:1]))
                seq.step("tensor", lambda tensor, g4=g4: (
                    _pv(tensor, g4) + (_scores(tensor, g4 + 1)
                                       if g4 < 3 else []))[-1])
            seq.step("vector", lambda v: v.reciprocal(dnr2[:],
                                                      pB[0:1, 512:1024]))
            seq.step("tensor", lambda tensor: [
                tensor.matmul(pA[:, 0:512], g1c128[0:1, :],
                              dnr2[:, :], start=True, stop=True),
                tensor.matmul(pA[:, 512:1024], gc128[0:1, :],
                              dnr[:, :], start=True, stop=True)][-1])
            seq.mstep("vector", lambda v: [
                v.tensor_copy(vq[:, 0:512], pA[:, 0:512]),
                v.tensor_mul(vq[:, 1024:1536], pA[:, 512:1024],
                             vq[:, 1536:2048])])
            seq.step("vector", lambda v: v.tensor_mul(vq[:, 0:512], vq[:, 0:512],
                                                      pB[:, 0:512]))
            seq.step("vector", lambda v: v.tensor_add(cmb[:, :], vq[:, 0:512],
                                                      vq[:, 1024:1536]))
            _pendst[0] = (hs, qc)

    seq.step("sync", lambda s: _takest(s), dma=True)

    for dv0 in range(2):
        seq.step("sync", lambda s, dv0=dv0: [s.dma_start(
            out=[wh, h1b][dv0][:, c * 2048:(c + 1) * 2048],
            in_=w2["o"][(dv0 * 8 + c) * 128:(dv0 * 8 + c + 1) * 128, :])
            for c in range(8)], dma=True)

    def _wo(dv):
        buf = [wh, h1b][dv // 8]
        return buf[:, (dv % 8) * 2048:(dv % 8 + 1) * 2048]

    cv3 = cb2.rearrange("(dvc pp) tt -> pp dvc tt", pp=128)
    for t in range(16):
        seq.step("sync", lambda s, t=t: [
            s.dma_start(out=xt[0][:, :], in_=xin[t * 128:(t + 1) * 128, :]),
            s.dma_start(out=tile_b[:, :].rearrange("pp (dvc j) -> pp dvc j",
                                                   dvc=16),
                        in_=cv3[:, :, t * 128:(t + 1) * 128])], dma=True)
        seq.step("tensor", lambda tensor: [
            tensor.matmul(pA[:, n * 512:(n + 1) * 512],
                          tile_b[:, dv * 128:(dv + 1) * 128],
                          _wo(dv)[:, n * 512:(n + 1) * 512],
                          start=(dv == 0), stop=(dv == 15))
            for dv in range(16) for n in range(4)][-1])
        seq.step("scalar", lambda s, t=t: s.activation(
            out=yt[:, :], in_=pA[:, :], func=AF.Copy, accum_out=ysum[:, t:t + 1]))
        seq.step("scalar", lambda s, t=t: s.activation(
            out=vq[:, :], in_=yt[:, :], func=AF.Square, accum_out=ysq[:, t:t + 1]))
        c = slice(t, t + 1)
        seq.step("vector", lambda v, c=c: v.tensor_scalar(
            ym[:, c], ysum[:, c], 1.0 / D, None, ALU.mult))
        seq.step("vector", lambda v, c=c: v.tensor_mul(yc1[:], ym[:, c], ym[:, c]))
        seq.step("vector", lambda v, c=c: v.scalar_tensor_tensor(
            yc1[:], ysq[:, c], 1.0 / D, yc1[:], ALU.mult, ALU.subtract))
        seq.step("scalar", lambda s: s.activation(out=yc2[:], in_=yc1[:],
                                                  func=AF.Sqrt, bias=epsc[:, 0:1]))
        seq.step("vector", lambda v, c=c: v.reciprocal(ysc[:, c], yc2[:]))
        seq.step("vector", lambda v, c=c: v.tensor_scalar(
            vq[:, :], yt[:, :], ym[:, c], ysc[:, c], ALU.subtract, ALU.mult))
        seq.step("vector", lambda v: v.tensor_add(yt[:, :], vq[:], xt[0][:, :]))
        seq.step("sync", lambda s, t=t: [s.dma_start(
            out=xout[t * 128:(t + 1) * 128, :], in_=yt[:, :])], dma=True)

    @block.sync
    def _(sync):
        sync.wait_ge(seq.s.h, seq.s.n)

    return _bfin(p, standalone)


def build_fused():
    p = Prog()
    nc = p.nc
    xin = p.din("xin", [2048, D])
    xout = p.dout("xout", [2048, D])
    x1 = p.dint("x1", [2048, D], F32)
    x2 = p.dint("x2", [2048, D], F32)
    x3 = p.dint("x3", [2048, D], F32)
    s_bar = p.sem("s_bar")
    fenceb = p.sbuf("fenceb", [1, 8], F32)
    block = p.es.enter_context(nc.Block())

    def barrier(n):
        @block.sync
        def _(sync):
            sync.dma_start(out=fenceb[0:1, 2:3], in_=fenceb[0:1, 3:4])\
                .then_inc(s_bar.h, 16)

        @block.vector
        def _(vector):
            vector.memset(fenceb[0:1, 0:1], 0.0).then_inc(s_bar.h, 1)

        @block.scalar
        def _(scalar):
            scalar.copy(fenceb[0:1, 1:2], fenceb[0:1, 1:2]).then_inc(s_bar.h, 1)

        @block.gpsimd
        def _(gpsimd):
            gpsimd.engine_nop().then_inc(s_bar.h, 1)

        tgt = 19 * n
        for mk in (block.sync, block.vector, block.scalar, block.tensor,
                   block.gpsimd):
            @mk
            def _(eng):
                eng.wait_ge(s_bar.h, tgt)

    build_l0_attn(p=p, block=block, xin=xin, xout=x1, pfx="A_")
    barrier(1)
    build_l0_ffn(p=p, block=block, xin=x1, xout=x2, pfx="B_")
    barrier(2)
    build_l1_attn(p=p, block=block, xin=x2, xout=x3, pfx="C_")
    barrier(3)
    build_l1_ffn(p=p, block=block, xin=x3, xout=xout, pfx="D_")
    return p.finish()


_IDENTB = np.eye(128).astype(ml_dtypes.bfloat16)
_NC_CACHE = {}


def _get_nc(name, builder):
    if name not in _NC_CACHE:
        _NC_CACHE[name] = builder()
    return _NC_CACHE[name]


_TVALS = np.arange(2048, dtype=np.float32).reshape(1, 2048)
_INVF = (1.0 / 10000.0 ** (np.arange(0, 2048, 2, dtype=np.float32) / 2048.0)).reshape(1, 1024)
_IDENTF = np.eye(128, dtype=np.float32)


def _l0_masks():
    CW = 64
    tp = np.zeros(128, dtype=np.int64)
    tp[:CW] = 2 * np.arange(CW)
    tp[CW:] = 2 * np.arange(CW) + 1
    msT = np.where(tp[:, None] < tp[None, :], 0.0, -1e30).astype(np.float32)
    ms = np.where(tp[None, :] < tp[:, None], 0.0, -1e30).astype(np.float32)
    minc = np.where(tp[:, None] <= (2 * np.arange(CW) + 1)[None, :],
                    0.0, -1e30).astype(np.float32)
    return msT, ms, minc


_MST, _MS, _MINC = _l0_masks()


def _maps_l0a(g):
    def rc(cw, nb):
        return np.ascontiguousarray(cw.reshape(nb, 128, 4).transpose(1, 0, 2)
                                    .reshape(128, nb * 4))
    return {"wqT": np.ascontiguousarray(g["L0_attn_Wq"].T),
            "wkT": np.ascontiguousarray(g["L0_attn_Wk"].T),
            "wvT": np.ascontiguousarray(g["L0_attn_Wv"].T),
            "wbT": np.ascontiguousarray(g["L0_attn_Wb"].T),
            "waT": np.ascontiguousarray(g["L0_attn_Wa"].T),
            "wgT": np.ascontiguousarray(g["L0_attn_Wg"].T),
            "woT": np.ascontiguousarray(g["L0_attn_Wo"].T),
            "alog": np.ascontiguousarray(g["L0_attn_A_log"].reshape(1, 8)),
            "dtb": np.ascontiguousarray(g["L0_attn_dt_bias"].reshape(8, 1)),
            "convq": rc(g["L0_attn_conv_q"], 8),
            "convk": rc(g["L0_attn_conv_k"], 16),
            "convv": rc(g["L0_attn_conv_v"], 16),
            "onw": np.ascontiguousarray(g["L0_attn_o_norm_w"].reshape(1, 128)),
            "msT_in": _MST, "ms_in": _MS, "minc_in": _MINC,
            "identb": _IDENTB, "identf_in": _IDENTF}


def _maps_l0f(g):
    return {"wcT": np.ascontiguousarray(g["L0_ffn_Wc"].T),
            "genes_in": np.ascontiguousarray(g["L0_ffn_genes"]),
            "wdT_in": np.ascontiguousarray(g["L0_ffn_Wd"].T),
            "wuT_in": np.ascontiguousarray(g["L0_ffn_Wu"].T),
            "nwcol_in": np.ascontiguousarray(g["L0_ffn_norm_w"].reshape(16, 128).T),
            "nwrow_in": np.ascontiguousarray(g["L0_ffn_norm_w"].reshape(1, 2048)),
            "tau_in": np.asarray(g["L0_ffn_temperature"], np.float32).reshape(1, 1),
            "scale_in": np.asarray(g["L0_ffn_scale"], np.float32).reshape(1, 1),
            "identb": _IDENTB}


def _maps_l1a(g):
    return {"wqT": np.ascontiguousarray(g["L1_attn_Wq"].T),
            "wkT": np.ascontiguousarray(g["L1_attn_Wk"].T),
            "wvT": np.ascontiguousarray(g["L1_attn_Wv"].T),
            "woT": np.ascontiguousarray(g["L1_attn_Wo"].T),
            "gate_in": np.ascontiguousarray(g["L1_attn_gate"].reshape(1, 16)),
            "tvals_in": _TVALS, "invf_in": _INVF,
            "identb": _IDENTB, "identf_in": _IDENTF}


def _maps_l1f(g):
    return {"wT": np.ascontiguousarray(g["L1_ffn_weight"].T),
            "identb": _IDENTB}


def run_l1_ffn(x, weight):
    nc = _get_nc("l1_ffn", build_l1_ffn)
    m = _maps_l1f({"L1_ffn_weight": weight})
    in_maps = [{"xin": np.ascontiguousarray(x[b]), **m}
               for b in range(x.shape[0])]
    res = run_bass_kernel_spmd(nc, in_maps, core_ids=list(range(x.shape[0])))
    return np.stack([r["xout"] for r in res.results])


def run_l0_ffn(x, g):
    nc = _get_nc("l0_ffn", build_l0_ffn)
    m = _maps_l0f(g)
    in_maps = [{"xin": np.ascontiguousarray(x[b]), **m}
               for b in range(x.shape[0])]
    res = run_bass_kernel_spmd(nc, in_maps, core_ids=list(range(x.shape[0])))
    return np.stack([r["xout"] for r in res.results])


def run_l1_attn(x, g):
    nc = _get_nc("l1_attn", build_l1_attn)
    m = _maps_l1a(g)
    maps = [{"xin": np.ascontiguousarray(x[b]), **m}
            for b in range(x.shape[0])]
    res = run_bass_kernel_spmd(nc, maps, core_ids=list(range(x.shape[0])))
    return np.stack([r["xout"] for r in res.results])


def run_l0_attn(x, g):
    nc = _get_nc("l0_attn", build_l0_attn)
    m = _maps_l0a(g)
    maps = [{"xin": np.ascontiguousarray(x[b]), **m}
            for b in range(x.shape[0])]
    res = run_bass_kernel_spmd(nc, maps, core_ids=list(range(x.shape[0])))
    return np.stack([r["xout"] for r in res.results])


def _pfx(d, pfx):
    return {pfx + k: v for k, v in d.items()}


def kernel_multi(**inputs):
    g = {k: np.asarray(v, np.float32) for k, v in inputs.items()}
    x = np.ascontiguousarray(g["L0_attn_x"])
    x = run_l0_attn(x, g)
    x = run_l0_ffn(x, g)
    x = run_l1_attn(x, g)
    x = run_l1_ffn(x, g["L1_ffn_weight"])
    return x


def kernel(**inputs):
    g = {k: np.asarray(v, np.float32) for k, v in inputs.items()}
    nc = _get_nc("fused", build_fused)
    base = {**_pfx(_maps_l0a(g), "A_"), **_pfx(_maps_l0f(g), "B_"),
            **_pfx(_maps_l1a(g), "C_"), **_pfx(_maps_l1f(g), "D_")}
    x = np.ascontiguousarray(g["L0_attn_x"])
    maps = [{**base, "xin": np.ascontiguousarray(x[b])}
            for b in range(x.shape[0])]
    res = run_bass_kernel_spmd(nc, maps, core_ids=list(range(x.shape[0])))
    return np.stack([r["xout"] for r in res.results])
